# version 01
# baseline (speedup 1.0000x reference)
import contextlib
import numpy as np
import concourse.bass as bass
import concourse.mybir as mybir
from concourse.bass_utils import run_bass_kernel_spmd

F32 = mybir.dt.float32
BF16 = mybir.dt.bfloat16
I32 = mybir.dt.int32
AF = mybir.ActivationFunctionType
ALU = mybir.AluOpType
AX = mybir.AxisListType

NCORES = 8
T = 2048
D = 2048
DIN = 4096
CONVD = 4224
NPROJ = 8352
NPROJB = 66
NH = 32
DST = 64
CHK = 256
NCHK = 8
DFF = 5504
FFB = 43
EPS_RMS = 1.1920929e-07
EPS_LN = 1e-5

MODE = "unfused"



def pack_lhsT(wt, pad_to=None):
    K, M = wt.shape
    if pad_to is not None and M < pad_to:
        wt = np.concatenate([wt, np.zeros((K, pad_to - M), wt.dtype)], axis=1)
        M = pad_to
    assert K % 128 == 0 and M % 128 == 0
    Kc, Mb = K // 128, M // 128
    t = wt.reshape(Kc, 128, Mb, 128).transpose(2, 1, 0, 3)
    return np.ascontiguousarray(t.reshape(Mb, 128, K))


def pack_fm(a):
    R, C = a.shape
    assert R % 128 == 0
    Rc = R // 128
    t = a.reshape(Rc, 128, C).transpose(1, 0, 2)
    return np.ascontiguousarray(t.reshape(128, Rc * C))


def _corebody(arm):
    try:
        return arm.logical % 8
    except Exception:
        return int(arm) % 8



class Prog:
    def __init__(self, fused):
        self.nc = bass.Bass("TRN2", target_bir_lowering=False)
        self.es = contextlib.ExitStack()
        self.fused = fused
        self.ext_in = {}
        self.ext_out = []
        self.aps = {}
        self.semv = {}
        self.semh = {}

    def inp(self, name, arrs, dtype=F32):
        if not isinstance(arrs, list):
            arrs = [arrs] * NCORES
        shape = list(arrs[0].shape)
        ap = self.nc.dram_tensor(name, shape, dtype, kind="ExternalInput").ap()
        self.ext_in[name] = arrs
        self.aps[name] = ap
        return ap

    def outp(self, name, shape, dtype=F32):
        ap = self.nc.dram_tensor(name, list(shape), dtype, kind="ExternalOutput").ap()
        self.ext_out.append(name)
        self.aps[name] = ap
        return ap

    def internal(self, name, shape, dtype=F32, shared=False):
        kw = {"addr_space": "Shared"} if shared else {}
        ap = self.nc.dram_tensor(name, list(shape), dtype, kind="Internal", **kw).ap()
        self.aps[name] = ap
        return ap

    def sb(self, name, shape, dtype):
        return self.es.enter_context(self.nc.sbuf_tensor(name, list(shape), dtype))

    def ps(self, name, shape, dtype=F32):
        return self.es.enter_context(self.nc.psum_tensor(name, list(shape), dtype))

    def sem(self, name):
        if name not in self.semh:
            self.semh[name] = self.es.enter_context(self.nc.semaphore(name))
            self.semv[name] = 0
        return self.semh[name]

    def bump(self, ins, name, n):
        h = self.sem(name)
        ins.then_inc(h, n)
        self.semv[name] += n
        return self.semv[name]

    def val(self, name):
        self.sem(name)
        return self.semv[name]

    def wait(self, eng, name, v=None):
        h = self.sem(name)
        eng.wait_ge(h, self.semv[name] if v is None else v)

    def run(self):
        in_maps = [{k: v[i] for k, v in self.ext_in.items()} for i in range(NCORES)]
        res = run_bass_kernel_spmd(self.nc, in_maps, core_ids=list(range(NCORES)))
        return [{k: r[k] for k in self.ext_out} for r in res.results]



def prep_inputs(i):
    p = {}
    x = np.asarray(i["L0_attn_x"], np.float32)
    xTs = []
    for c in range(NCORES):
        b, h = c >> 1, c & 1
        sl = x[b, h * T:(h + 1) * T, :].T
        halo = np.zeros((D, 3), np.float32) if h == 0 else x[b, h * T - 3:h * T, :].T
        xTs.append(np.ascontiguousarray(np.concatenate([halo, sl], axis=1)))
    p["xT"] = xTs
    p["w_in_p"] = pack_lhsT(np.ascontiguousarray(np.asarray(i["L0_attn_W_in"], np.float32).T), pad_to=NPROJB * 128)
    cw = np.asarray(i["L0_attn_conv_w"], np.float32)
    p["convw_pk"] = np.ascontiguousarray(cw.T.reshape(4, 33, 128).transpose(0, 2, 1))
    p["convb_row"] = np.ascontiguousarray(np.asarray(i["L0_attn_conv_b"], np.float32)[None, :])
    p["A_log"] = np.ascontiguousarray(np.asarray(i["L0_attn_A_log"], np.float32)[:, None])
    p["dt_bias"] = np.ascontiguousarray(np.asarray(i["L0_attn_dt_bias"], np.float32)[:, None])
    p["D_row"] = np.ascontiguousarray(np.asarray(i["L0_attn_D"], np.float32)[None, :])
    p["w_out_p"] = pack_lhsT(np.ascontiguousarray(np.asarray(i["L0_attn_W_out"], np.float32).T))
    p["l0_w_router"] = pack_fm(np.asarray(i["L0_ffn_W_router"], np.float32))
    p["l0_w_gate_p"] = pack_lhsT(np.asarray(i["L0_ffn_W_gate"], np.float32), pad_to=FFB * 128)
    p["l0_w_up_p"] = pack_lhsT(np.asarray(i["L0_ffn_W_up"], np.float32), pad_to=FFB * 128)
    wd = np.asarray(i["L0_ffn_W_down"], np.float32)
    wd_pad = np.concatenate([wd, np.zeros((FFB * 128 - DFF, D), np.float32)], axis=0)
    p["l0_w_down_p"] = pack_lhsT(wd_pad)
    p["l0_sA_gate"] = pack_fm(np.asarray(i["L0_ffn_sA_gate"], np.float32))
    p["l0_sA_up"] = pack_fm(np.asarray(i["L0_ffn_sA_up"], np.float32))
    sad = np.asarray(i["L0_ffn_sA_down"], np.float32)
    sad = np.concatenate([sad, np.zeros((FFB * 128 - DFF, 8), np.float32)], axis=0)
    p["l0_sA_down"] = pack_fm(sad)
    p["l0_sB_gate"] = np.ascontiguousarray(np.asarray(i["L0_ffn_sB_gate"], np.float32))
    p["l0_sB_up"] = np.ascontiguousarray(np.asarray(i["L0_ffn_sB_up"], np.float32))
    p["l0_sB_down"] = np.ascontiguousarray(np.asarray(i["L0_ffn_sB_down"], np.float32))
    p["l0_A_gate_all"] = pack_fm(np.ascontiguousarray(
        np.asarray(i["L0_ffn_A_gate"], np.float32).transpose(1, 0, 2).reshape(D, 64)))
    p["l0_A_up_all"] = pack_fm(np.ascontiguousarray(
        np.asarray(i["L0_ffn_A_up"], np.float32).transpose(1, 0, 2).reshape(D, 64)))
    adl = np.ascontiguousarray(np.asarray(i["L0_ffn_A_down"], np.float32).transpose(1, 0, 2).reshape(DFF, 64))
    adl = np.concatenate([adl, np.zeros((FFB * 128 - DFF, 64), np.float32)], axis=0)
    p["l0_A_down_all"] = pack_fm(adl)
    p["l0_B_gate_all"] = np.ascontiguousarray(np.asarray(i["L0_ffn_B_gate"], np.float32).reshape(64, DFF))
    p["l0_B_up_all"] = np.ascontiguousarray(np.asarray(i["L0_ffn_B_up"], np.float32).reshape(64, DFF))
    p["l0_B_down_all"] = np.ascontiguousarray(np.asarray(i["L0_ffn_B_down"], np.float32).reshape(64, D))
    p["l0_ebias_row"] = np.ascontiguousarray(np.asarray(i["L0_ffn_expert_bias"], np.float32)[None, :])
    qkv = np.asarray(i["L1_attn_qkv"], np.float32)
    p["l1_qk_p"] = pack_lhsT(np.ascontiguousarray(qkv[:, :4096]))
    p["l1_v_fm"] = pack_fm(np.ascontiguousarray(qkv[:, 4096:]))
    p["l1_oproj_p"] = pack_lhsT(np.asarray(i["L1_attn_o_proj"], np.float32))
    p["l1_gamma"] = np.ascontiguousarray(np.asarray(i["L1_attn_gamma"], np.float32).reshape(16, 128).T)
    p["l1_beta"] = np.ascontiguousarray(np.asarray(i["L1_attn_beta"], np.float32).reshape(16, 128).T)
    p["l1_router"] = pack_fm(np.ascontiguousarray(np.asarray(i["L1_ffn_router_w"], np.float32).T))
    w1 = np.asarray(i["L1_ffn_w1"], np.float32)
    w3 = np.asarray(i["L1_ffn_w3"], np.float32)
    w2 = np.asarray(i["L1_ffn_w2"], np.float32)
    p["l1_w1_p"] = np.ascontiguousarray(np.stack([pack_lhsT(w1[e]) for e in range(8)]))
    p["l1_w3_p"] = np.ascontiguousarray(np.stack([pack_lhsT(w3[e]) for e in range(8)]))
    p["l1_w2_p"] = np.ascontiguousarray(np.stack([pack_lhsT(w2[e]) for e in range(8)]))
    p["h_val"] = [np.full((128, 1), float(c & 1), np.float32) for c in range(NCORES)]
    p["carry_mask"] = [np.full((64, 1), float(c & 1), np.float32) for c in range(NCORES)]
    return p



def build_p0(P, xT, w_in_p, dt_bias, zb, xbc, dtf):
    nc = P.nc
    TT = [(0, 3), (3, 512), (515, 512), (1027, 512), (1539, 512)]
    x_sb = P.sb("p0_x", [128, 16 * 2051], BF16)
    wbuf = [P.sb(f"p0_w{j}", [128, 2048], BF16) for j in range(2)]
    ob = [P.sb(f"p0_o{j}", [128, 512], BF16) for j in range(4)]
    ohalo = P.sb("p0_oh", [128, 3], BF16)
    dtraw = P.sb("p0_dtraw", [32, 2048], F32)
    dtb_sb = P.sb("p0_dtb", [32, 1], F32)
    dt_t1 = P.sb("p0_dt1", [32, 2048], F32)
    dt_t2 = P.sb("p0_dt2", [32, 2048], F32)
    pb = [P.ps(f"p0_ps{j}", [128, 512], F32) for j in range(2)]
    ph = P.ps("p0_psh", [128, 3], F32)

    def tiles_of(mb):
        if mb < 32 or mb == 65:
            return TT[1:]
        return TT

    with nc.Block() as blk:
        @blk.gpsimd
        def _(g):
            for kc in range(16):
                ins = g.dma_start(out=x_sb[:, kc * 2051:(kc + 1) * 2051],
                                  in_=xT[kc * 128:(kc + 1) * 128, :])
                P.bump(ins, "xld", 16)
            P.bump(g.dma_start(out=dtb_sb[:, :], in_=dt_bias[:, :]), "xld", 16)
            for mb in range(NPROJB):
                if mb >= 2:
                    g.wait_ge(P.sem("evac"), 4 * (mb - 1))
                    hc = max(0, min(mb - 1, 65) - 32)
                    if hc > 0:
                        g.wait_ge(P.sem("hevac"), hc)
                P.bump(g.dma_start(out=wbuf[mb % 2][:, :], in_=w_in_p[mb]), f"wld{mb % 2}", 16)

        @blk.tensor
        def t_(t):
            t.wait_ge(P.sem("xld"), 17 * 16)
            gctr = 0
            hctr = 0
            for mb in range(NPROJB):
                t.wait_ge(P.sem(f"wld{mb % 2}"), 16 * (mb // 2 + 1))
                for (off, w) in tiles_of(mb):
                    if w == 3:
                        if hctr >= 1:
                            t.wait_ge(P.sem("hevac"), hctr)
                        dst = ph
                    else:
                        if gctr >= 2:
                            t.wait_ge(P.sem("evac"), gctr - 1)
                        dst = pb[gctr % 2]
                    for kc in range(16):
                        ins = t.matmul(dst[:, 0:w], wbuf[mb % 2][:, kc * 128:(kc + 1) * 128],
                                       x_sb[:, kc * 2051 + off: kc * 2051 + off + w],
                                       start=(kc == 0), stop=(kc == 15))
                    if w == 3:
                        hctr += 1
                        P.bump(ins, "hmm", 1)
                    else:
                        gctr += 1
                        P.bump(ins, "gmm", 1)

        @blk.vector
        def v_(v):
            gctr = 0
            hctr = 0
            for mb in range(NPROJB):
                is_dt = mb == 65
                for (off, w) in tiles_of(mb):
                    if w == 3:
                        hctr += 1
                        v.wait_ge(P.sem("hmm"), hctr)
                        if hctr >= 2:
                            v.wait_ge(P.sem("hstore"), 16 * (hctr - 1))
                        ins = v.tensor_copy(ohalo[:, :], ph[:, :])
                        P.bump(ins, "hevac", 1)
                    else:
                        gctr += 1
                        v.wait_ge(P.sem("gmm"), gctr)
                        if is_dt:
                            ins = v.tensor_copy(dtraw[:, off - 3: off - 3 + w],
                                                pb[(gctr - 1) % 2][0:32, :])
                            P.bump(ins, "evac", 1)
                            P.bump(v.nop(), f"store{(gctr - 1) % 4}", 16)
                        else:
                            if gctr > 4:
                                v.wait_ge(P.sem(f"store{(gctr - 1) % 4}"), 16 * ((gctr - 1) // 4))
                            ins = v.tensor_copy(ob[(gctr - 1) % 4][:, :], pb[(gctr - 1) % 2][:, :])
                            P.bump(ins, "evac", 1)

        @blk.sync
        def s_(s):
            gctr = 0
            hctr = 0
            for mb in range(NPROJB):
                is_z = mb < 32
                is_dt = mb == 65
                for (off, w) in tiles_of(mb):
                    if w == 3:
                        hctr += 1
                        s.wait_ge(P.sem("hevac"), hctr)
                        P.bump(s.dma_start(out=xbc[(mb - 32) * 128:(mb - 31) * 128, 0:3],
                                           in_=ohalo[:, :]), "hstore", 16)
                    else:
                        gctr += 1
                        if is_dt:
                            continue
                        s.wait_ge(P.sem("evac"), gctr)
                        if is_z:
                            dst = zb[mb * 128:(mb + 1) * 128, off - 3:off - 3 + w]
                        else:
                            dst = xbc[(mb - 32) * 128:(mb - 31) * 128, off:off + w]
                        P.bump(s.dma_start(out=dst, in_=ob[(gctr - 1) % 4][:, :]),
                               f"store{(gctr - 1) % 4}", 16)

        @blk.scalar
        def sc_(sc):
            sc.wait_ge(P.sem("evac"), P.semv["evac"])
            P.bump(sc.activation(dt_t1[:, :], dtraw[:, :], AF.Identity, bias=dtb_sb[:, :]), "dt1", 1)

        @blk.scalar
        def sc2_(sc):
            P.wait(sc, "dt1")
            sc.activation(dt_t2[:, :], dt_t1[:, :], AF.Abs)
            sc.activation(dt_t2[:, :], dt_t2[:, :], AF.Exp, scale=-1.0)
            P.bump(sc.activation(dt_t2[:, :], dt_t2[:, :], AF.Ln, bias=1.0), "dt3", 1)

        @blk.vector
        def v3_(v):
            P.wait(v, "dt3")
            v.tensor_scalar(dt_t1[:, :], dt_t1[:, :], 0.0, None, ALU.max)
            P.bump(v.tensor_tensor(dt_t1[:, :], dt_t1[:, :], dt_t2[:, :], ALU.add), "dt4", 1)

        @blk.gpsimd
        def g2_(g):
            P.wait(g, "dt4")
            for j in range(4):
                P.wait(g, f"store{j}")
            P.wait(g, "hstore")
            P.bump(g.dma_start(out=dtf[:, :], in_=dt_t1[:, :]), "fstore", 16)
            P.wait(g, "fstore")
    return P


def run_phase_p0(prep, fused=False):
    P = Prog(fused)
    xT = P.inp("xT", prep["xT"])
    w_in_p = P.inp("w_in_p", prep["w_in_p"])
    dt_bias = P.inp("dt_bias", prep["dt_bias"])
    zb = P.outp("zb", (DIN, T), BF16)
    xbc = P.outp("xbc", (CONVD, 3 + T), BF16)
    dtf = P.outp("dtf", (NH, T), F32)
    build_p0(P, xT, w_in_p, dt_bias, zb, xbc, dtf)
    return P.run()



def build_p0b(P, xbc, convw_pk, convb_row, xcb):
    nc = P.nc
    NT = 4
    xbc_sb = P.sb("pc_x", [128, 33 * 2051], BF16)
    cw = [P.sb(f"pc_cw{k}", [128, 33], F32) for k in range(4)]
    cbrow = P.sb("pc_cb", [1, CONVD], BF16)
    onesrow = P.sb("pc_ones", [1, 512], BF16)
    iot = P.sb("pc_iota", [128, 128], BF16)
    idb = P.sb("pc_id", [128, 128], BF16)
    dk = [P.sb(f"pc_dk{k}", [128, 128], BF16) for k in range(4)]
    ob = [P.sb(f"pc_o{j}", [128, 512], BF16) for j in range(4)]
    pb = [P.ps(f"pc_ps{j}", [128, 512], F32) for j in range(2)]

    with nc.Block() as blk:
        @blk.gpsimd
        def _(g):
            for cc in range(33):
                P.bump(g.dma_start(out=xbc_sb[:, cc * 2051:(cc + 1) * 2051],
                                   in_=xbc[cc * 128:(cc + 1) * 128, :]), "xld", 16)
            for k in range(4):
                P.bump(g.dma_start(out=cw[k][:, :], in_=convw_pk[k]), "xld", 16)
            P.bump(g.dma_start(out=cbrow[:, :], in_=convb_row[:, :]), "xld", 16)
            g.iota(iot[:, :], [[1, 128]], channel_multiplier=-1,
                   allow_small_or_imprecise_dtypes=True)
            P.bump(g.memset(onesrow[:, :], 1.0), "ild", 1)

        @blk.vector
        def v0_(v):
            P.wait(v, "ild")
            P.bump(v.tensor_scalar(idb[:, :], iot[:, :], 0.0, None, ALU.is_equal), "idd", 1)

        @blk.vector
        def v1_(v):
            g = 0
            for cc in range(33):
                v.wait_ge(P.sem("idd"), 1)
                v.wait_ge(P.sem("xld"), 38 * 16)
                for k in range(4):
                    ins = v.tensor_scalar(dk[k][:, :], idb[:, :], cw[k][:, cc:cc + 1], None, ALU.mult)
                P.bump(ins, "dk", 1)
                for tt in range(NT):
                    g += 1
                    v.wait_ge(P.sem("mm"), g)
                    if g > 4:
                        v.wait_ge(P.sem(f"st{(g - 1) % 4}"), 16 * ((g - 1) // 4))
                    ins = v.tensor_copy(ob[(g - 1) % 4][:, :], pb[(g - 1) % 2][:, :])
                    P.bump(ins, "evac", 1)

        @blk.scalar
        def sc_(sc):
            g = 0
            for cc in range(33):
                for tt in range(NT):
                    g += 1
                    sc.wait_ge(P.sem("evac"), g)
                    P.bump(sc.activation(ob[(g - 1) % 4][:, :], ob[(g - 1) % 4][:, :], AF.Silu),
                           "silu", 1)

        @blk.tensor
        def t_(t):
            t.wait_ge(P.sem("xld"), 38 * 16)
            g = 0
            for cc in range(33):
                t.wait_ge(P.sem("dk"), cc + 1)
                for tt in range(NT):
                    g += 1
                    if g > 2:
                        t.wait_ge(P.sem("evac"), g - 2)
                    ps = pb[(g - 1) % 2]
                    for k in range(4):
                        t.matmul(ps[:, :], dk[k][:, :],
                                 xbc_sb[:, cc * 2051 + tt * 512 + k: cc * 2051 + tt * 512 + k + 512],
                                 start=(k == 0), stop=False)
                    ins = t.matmul(ps[:, :], cbrow[:, cc * 128:(cc + 1) * 128],
                                   onesrow[:, :], start=False, stop=True)
                    P.bump(ins, "mm", 1)

        @blk.sync
        def s_(s):
            g = 0
            for cc in range(33):
                for tt in range(NT):
                    g += 1
                    s.wait_ge(P.sem("silu"), g)
                    ins = s.dma_start(out=xcb[cc * 128:(cc + 1) * 128, tt * 512:(tt + 1) * 512],
                                      in_=ob[(g - 1) % 4][:, :])
                    P.bump(ins, f"st{(g - 1) % 4}", 16)

        @blk.gpsimd
        def g2_(g):
            for j in range(4):
                P.wait(g, f"st{j}")
    return P


def run_phase_p0b(prep, xbc_arrs, fused=False):
    P = Prog(fused)
    xbc = P.inp("xbc", xbc_arrs, BF16)
    convw_pk = P.inp("convw_pk", prep["convw_pk"])
    convb_row = P.inp("convb_row", prep["convb_row"])
    xcb = P.outp("xcb", (CONVD, T), BF16)
    build_p0b(P, xbc, convw_pk, convb_row, xcb)
    return P.run()



def build_p1(P, xcb, dtf, A_log, D_row, ypart, chq, carry_out, lgrow):
    nc = P.nc
    dt_sb = P.sb("p1_dt", [32, T], F32)
    dA = P.sb("p1_dA", [32, T], F32)
    Acol = P.sb("p1_A", [32, 1], F32)
    Drow_sb = P.sb("p1_Drow", [1, 32], BF16)
    D128 = P.sb("p1_D128", [128, 32], F32)
    Dident = P.sb("p1_Did", [128, 32 * 128], BF16)
    iot = P.sb("p1_iota", [128, 256], F32)
    idf = P.sb("p1_idf", [128, 128], F32)
    idb = P.sb("p1_idb", [128, 128], BF16)
    tri = P.sb("p1_tri", [128, 128], F32)
    onesq = P.sb("p1_onesq", [128, 128], F32)
    ntri = P.sb("p1_ntri", [128, 128], F32)
    nonesq = P.sb("p1_nonesq", [128, 128], F32)
    ones1f = P.sb("p1_o1f", [1, 128], F32)
    ones1b = P.sb("p1_o1b", [1, 128], BF16)
    iotasel = P.sb("p1_iotasel", [32, 128], F32)
    sel = P.sb("p1_sel", [32, 32 * 128], F32)
    sel64 = P.sb("p1_sel64", [32, 32 * 64], BF16)
    mask01 = P.sb("p1_mask", [128, 512], BF16)
    carry = P.sb("p1_carry", [64, 32 * 128], BF16)
    lg = P.sb("p1_lg", [32, 1], F32)
    dAT = P.sb("p1_dAT", [128, 64], F32)
    acsT = P.sb("p1_acsT", [128, 64], F32)
    nacsT = P.sb("p1_nacsT", [128, 64], F32)
    acs_fm = P.sb("p1_acsfm", [32, 256], F32)
    Eacs_fm = P.sb("p1_Eacs", [32, 256], BF16)
    alast = P.sb("p1_alast", [32, 1], F32)
    alast_row = P.sb("p1_alrow", [1, 32], F32)
    cdrow = P.sb("p1_cdrow", [1, 32], F32)
    CD64 = P.sb("p1_cd64", [64, 32], F32)
    decT = P.sb("p1_decT", [128, 64], F32)
    dtT = P.sb("p1_dtT", [128, 64], F32)
    lgrow_sb = P.sb("p1_lgrow", [1, 32], F32)
    CBT = P.sb("p1_cbt", [128, 512], BF16)
    MT_all = P.sb("p1_MT", [128, 32 * 512], BF16)
    xd_all = P.sb("p1_xd", [128, 32 * 256], BF16)
    btdd_all = P.sb("p1_btdd", [128, 32 * 128], BF16)
    chq_sb = P.sb("p1_chq", [64, 32 * 256], BF16)
    y_sb = P.sb("p1_y", [128, 32 * 256], BF16)
    diff_sb = [P.sb(f"p1_diff{j}", [128, 512], F32) for j in range(2)]
    exp_sb = [P.sb(f"p1_exp{j}", [128, 512], BF16) for j in range(2)]
    xfm = [P.sb(f"p1_xfm{j}", [128, 32 * 256], BF16) for j in range(2)]
    Bfm = [P.sb(f"p1_Bfm{j}", [64, 256], BF16) for j in range(2)]
    Cfm = [P.sb(f"p1_Cfm{j}", [64, 256], BF16) for j in range(2)]
    xt = [P.sb(f"p1_xt_{k}", [128, 4096], BF16) for k in range(2)]
    bt = [[P.sb(f"p1_bt{j}_{k}", [128, 64], BF16) for k in range(2)] for j in range(2)]
    ps_pre = P.ps("p1_pspre", [128, 512], F32)
    ps_tr = ps_pre[:, 0:64]
    ps_tr2 = ps_pre[:, 64:192]
    ps_cbt = ps_pre[:, 192:448]
    ps_a = [P.ps(f"p1_psa{j}", [128, 256], F32) for j in range(2)]
    ps_e = [P.ps(f"p1_pse{j}", [64, 256], F32) for j in range(2)]
    ps_y = [P.ps(f"p1_psy{j}", [128, 256], F32) for j in range(2)]
    ps_s = P.ps("p1_pss", [64, 128], F32)

    LD_PER_CHUNK = 7 * 16

    with nc.Block() as blk:
        @blk.sync
        def s0_(s):
            P.bump(s.dma_start(out=dt_sb[:, :], in_=dtf[:, :]), "ild", 16)
            P.bump(s.dma_start(out=Acol[:, :], in_=A_log[:, :]), "ild", 16)

        @blk.gpsimd
        def g0_(g):
            P.bump(g.dma_start(out=Drow_sb[:, :], in_=D_row[:, :]), "ildg", 16)
            g.iota(iot[:, :], [[1, 256]], channel_multiplier=-1,
                   allow_small_or_imprecise_dtypes=True)
            g.iota(iotasel[:, :], [[0, 128]], channel_multiplier=1,
                   allow_small_or_imprecise_dtypes=True)
            g.memset(ones1f[:, :], 1.0)
            g.memset(ones1b[:, :], 1.0)
            g.memset(onesq[:, :], 1.0)
            g.memset(nonesq[:, :], -1.0)
            g.memset(carry[:, :], 0.0)
            P.bump(g.memset(lg[:, :], 0.0), "icst", 1)

        @blk.vector
        def v0_(v):
            P.wait(v, "icst")
            P.wait(v, "ild")
            P.wait(v, "ildg")
            v.tensor_scalar(idf[:, :], iot[:, 0:128], 0.0, None, ALU.is_equal)
            v.tensor_copy(idb[:, :], idf[:, :])
            v.tensor_scalar(tri[:, :], iot[:, 0:128], 0.0, None, ALU.is_ge)
            v.tensor_scalar(ntri[:, :], tri[:, :], -1.0, None, ALU.mult)
            v.tensor_scalar(mask01[:, 0:256], iot[:, :], 0.0, None, ALU.is_ge)
            v.tensor_scalar(mask01[:, 256:512], iot[:, :], 128.0, None, ALU.is_ge)
            for h in range(32):
                v.tensor_scalar(sel[:, h * 128:(h + 1) * 128], iotasel[:, :],
                                float(h), None, ALU.is_equal)
                v.tensor_scalar(sel64[:, h * 64:(h + 1) * 64], iotasel[:, 0:64],
                                float(h), None, ALU.is_equal)
            P.bump(v.nop(), "cst1", 1)

        @blk.scalar
        def sc0_(sc):
            P.wait(sc, "cst1")
            P.bump(sc.activation(Acol[:, :], Acol[:, :], AF.Exp), "cst2", 1)

        @blk.vector
        def v1_(v):
            P.wait(v, "cst2")
            P.bump(v.tensor_scalar(Acol[:, :], Acol[:, :], -1.0, None, ALU.mult), "acneg", 1)
            v.wait_ge(P.sem("acneg"), 1)
            P.bump(v.tensor_scalar(dA[:, :], dt_sb[:, :], Acol[:, :], None, ALU.mult), "cst3", 1)

        @blk.tensor
        def t0_(t):
            P.wait(t, "icst")
            P.wait(t, "ildg")
            P.bump(t.matmul(ps_tr2[:, 0:32], ones1b[:, :], Drow_sb[:, :],
                            start=True, stop=True), "dmm", 1)

        @blk.vector
        def v2_(v):
            P.wait(v, "dmm")
            P.bump(v.tensor_copy(D128[:, :], ps_tr2[:, 0:32]), "d128c", 1)
            v.wait_ge(P.sem("d128c"), 1)
            for h in range(32):
                ins = v.tensor_scalar(Dident[:, h * 128:(h + 1) * 128], idb[:, :],
                                      D128[:, h:h + 1], None, ALU.mult)
            P.bump(ins, "dident", 1)

        for c in range(NCHK):
            cb = c % 2
            c0 = c * 256

            @blk.sync
            def ld_(s, c=c, cb=cb, c0=c0):
                if c >= 2:
                    s.wait_ge(P.sem("pb1"), 32 * (c - 1))
                    s.wait_ge(P.sem("dmt"), 32 * (c - 1))
                    s.wait_ge(P.sem("dchq"), 32 * (c - 1))
                nld = f"ld{cb}"
                P.bump(s.dma_start(
                    out=xfm[cb][:, :].rearrange("p (pc t) -> p pc t", pc=32),
                    in_=xcb[0:DIN, c0:c0 + 256].rearrange("(pc p) t -> p pc t", p=128)), nld, 16)
                P.bump(s.dma_start(out=Bfm[cb][:, :], in_=xcb[DIN:DIN + 64, c0:c0 + 256]), nld, 16)
                P.bump(s.dma_start(out=Cfm[cb][:, :], in_=xcb[DIN + 64:DIN + 128, c0:c0 + 256]), nld, 16)
                if c >= 1:
                    s.wait_ge(P.sem("dmt"), 32 * c)
                for kt in range(2):
                    P.bump(s.dma_start(out=xt[kt][:, :],
                                       in_=xcb[0:DIN, c0 + kt * 128:c0 + (kt + 1) * 128],
                                       transpose=True), nld, 16)
                    P.bump(s.dma_start(out=bt[cb][kt][:, :],
                                       in_=xcb[DIN:DIN + 64, c0 + kt * 128:c0 + (kt + 1) * 128],
                                       transpose=True), nld, 16)

            @blk.tensor
            def pt1_(t, c=c, c0=c0):
                P.wait(t, "cst3")
                if c == 0:
                    P.wait(t, "dident")
                if c > 0:
                    P.wait(t, "pre_dve")
                    t.wait_ge(P.sem("pre_dve4"), c)
                t.transpose(ps_tr2[:, 0:32], dA[:, c0:c0 + 128], idf[0:32, 0:32])
                t.transpose(ps_tr2[:, 32:64], dA[:, c0 + 128:c0 + 256], idf[0:32, 0:32])
                ins = t.transpose(ps_tr2[0:1, 64:96], lg[:, :], idf[0:32, 0:32])
                P.bump(ins, "pre_pe", 1)

            @blk.scalar
            def pa1_(sc, c=c):
                P.wait(sc, "pre_pe")
                if c > 0:
                    sc.wait_ge(P.sem("lgst"), 16 * c)
                sc.copy(dAT[:, 0:32], ps_tr2[:, 0:32])
                sc.copy(dAT[:, 32:64], ps_tr2[:, 32:64])
                ins = sc.copy(lgrow_sb[:, :], ps_tr2[0:1, 64:96])
                P.bump(ins, "pre_act", 1)

            @blk.vector
            def pv1_(v, c=c, c0=c0):
                P.wait(v, "pre_act")
                v.reduce_sum(alast[:, :], dA[:, c0:c0 + 256], axis=AX.X)
                ins = v.tensor_tensor(lg[:, :], lg[:, :], alast[:, :], ALU.add)
                P.bump(ins, "pre_dve", 1)

            @blk.gpsimd
            def pg1_(g, c=c):
                P.wait(g, "pre_act")
                P.bump(g.dma_start(out=lgrow[c:c + 1, :], in_=lgrow_sb[:, :]), "lgst", 16)

            @blk.tensor
            def pt2_(t):
                P.wait(t, "pre_dve")
                t.transpose(ps_tr2[0:1, 96:128], alast[:, :], idf[0:32, 0:32])
                t.matmul(ps_tr[:, 0:32], tri[:, :], dAT[:, 0:32], start=True, stop=True)
                t.matmul(ps_tr[:, 32:64], onesq[:, :], dAT[:, 0:32], start=True, stop=False)
                ins = t.matmul(ps_tr[:, 32:64], tri[:, :], dAT[:, 32:64], start=False, stop=True)
                P.bump(ins, "pre_pe2", 1)

            @blk.scalar
            def pa2_(sc, c=c):
                P.wait(sc, "pre_pe2")
                if c > 0:
                    sc.wait_ge(P.sem("dcl"), 32 * c)
                sc.copy(acsT[:, 0:32], ps_tr[:, 0:32])
                sc.copy(acsT[:, 32:64], ps_tr[:, 32:64])
                sc.mul(nacsT[:, 0:64], acsT[:, 0:64], -1.0)
                ins = sc.copy(alast_row[:, :], ps_tr2[0:1, 96:128])
                P.bump(ins, "pre_act2", 1)

            @blk.tensor
            def pt3_(t):
                P.wait(t, "pre_act2")
                t.transpose(ps_cbt[0:32, 0:128], acsT[:, 0:32], idf[:, :])
                ins = t.transpose(ps_cbt[0:32, 128:256], acsT[:, 32:64], idf[:, :])
                P.bump(ins, "pre_pe3", 1)

            @blk.scalar
            def pa3_(sc, c=c):
                P.wait(sc, "pre_pe3")
                if c > 0:
                    sc.wait_ge(P.sem("pae"), 32 * c)
                sc.copy(acs_fm[:, 0:128], ps_cbt[0:32, 0:128])
                sc.copy(acs_fm[:, 128:256], ps_cbt[0:32, 128:256])
                sc.activation(Eacs_fm[:, :], acs_fm[:, :], AF.Exp)
                ins = sc.activation(cdrow[:, :], alast_row[:, :], AF.Exp)
                P.bump(ins, "pre_act3", 1)

            @blk.tensor
            def pt4_(t):
                P.wait(t, "pre_act3")
                t.matmul(ps_tr2[0:64, 0:32], ones1f[0:1, 0:64], cdrow[:, :], start=True, stop=True)
                t.matmul(ps_tr[:, 0:32], ones1f[:, :], alast_row[:, :], start=True, stop=False)
                t.matmul(ps_tr[:, 0:32], ntri[:, :], dAT[:, 0:32], start=False, stop=True)
                t.matmul(ps_tr[:, 32:64], ones1f[:, :], alast_row[:, :], start=True, stop=False)
                t.matmul(ps_tr[:, 32:64], nonesq[:, :], dAT[:, 0:32], start=False, stop=False)
                ins = t.matmul(ps_tr[:, 32:64], ntri[:, :], dAT[:, 32:64], start=False, stop=True)
                P.bump(ins, "pre_pe4", 1)

            @blk.scalar
            def pa4_(sc, c=c):
                P.wait(sc, "pre_pe4")
                if c > 0:
                    sc.wait_ge(P.sem("dscan"), 32 * c)
                sc.copy(CD64[:, :], ps_tr2[0:64, 0:32])
                ins = sc.activation(decT[:, 0:64], ps_tr[:, 0:64], AF.Exp)
                P.bump(ins, "pre_act4", 1)

            @blk.tensor
            def pt5_(t, c0=c0):
                P.wait(t, "pre_act4")
                t.transpose(ps_tr2[:, 0:32], dt_sb[:, c0:c0 + 128], idf[0:32, 0:32])
                ins = t.transpose(ps_tr2[:, 32:64], dt_sb[:, c0 + 128:c0 + 256], idf[0:32, 0:32])
                P.bump(ins, "pre_pe5", 1)

            @blk.scalar
            def pa5_(sc, c=c):
                P.wait(sc, "pre_pe5")
                if c > 0:
                    sc.wait_ge(P.sem("dmt"), 32 * c)
                ins = sc.copy(dtT[:, 0:64], ps_tr2[:, 0:64])
                P.bump(ins, "pre_act5", 1)

            @blk.tensor
            def pt6_(t, c=c, cb=cb):
                t.wait_ge(P.sem(f"ld{cb}"), LD_PER_CHUNK * (c // 2 + 1))
                P.wait(t, "dident")
                P.wait(t, "pre_act5")
                P.bump(t.matmul(ps_cbt[:, :], Bfm[cb][:, 0:128], Cfm[cb][:, :],
                                start=True, stop=True), "pre_pe6", 1)

            @blk.vector
            def pv3_(v):
                P.wait(v, "pre_pe6")
                ins = v.tensor_tensor(CBT[:, 0:256], ps_cbt[:, :], mask01[:, 0:256], ALU.mult)
                P.bump(ins, "pre_dve3", 1)

            @blk.tensor
            def pt7_(t, cb=cb):
                P.wait(t, "pre_dve3")
                P.bump(t.matmul(ps_cbt[:, :], Bfm[cb][:, 128:256], Cfm[cb][:, :],
                                start=True, stop=True), "pre_pe7", 1)

            @blk.vector
            def pv4_(v):
                P.wait(v, "pre_pe7")
                ins = v.tensor_tensor(CBT[:, 256:512], ps_cbt[:, :], mask01[:, 256:512], ALU.mult)
                P.bump(ins, "pre_dve4", 1)

            base = c * 32

            @blk.tensor
            def ta_(t, c=c, base=base):
                P.wait(t, "pre_act3")
                for h in range(32):
                    gh = base + h
                    if gh >= 2:
                        t.wait_ge(P.sem("dcl"), gh - 1)
                        t.wait_ge(P.sem("dchq"), gh - 1)
                    P.bump(t.matmul(ps_a[h % 2][:, :], sel[:, h * 128:(h + 1) * 128],
                                    acs_fm[:, :], start=True, stop=True), "paa", 1)
                    P.bump(t.matmul(ps_e[h % 2][:, :], sel64[:, h * 64:(h + 1) * 64],
                                    Eacs_fm[:, :], start=True, stop=True), "pae", 1)

            @blk.scalar
            def aa_(sc, base=base):
                for h in range(32):
                    gh = base + h
                    sc.wait_ge(P.sem("dcl"), gh + 1)
                    if gh >= 2:
                        sc.wait_ge(P.sem("dmt"), gh - 1)
                    sc.activation(exp_sb[h % 2][:, 0:256], diff_sb[h % 2][:, 0:256], AF.Exp)
                    P.bump(sc.activation(exp_sb[h % 2][:, 256:512], diff_sb[h % 2][:, 256:512],
                                         AF.Exp), "aexp", 1)

            @blk.vector
            def va_(v, c=c, cb=cb, base=base):
                v.wait_ge(P.sem("pre_act5"), c + 1)
                v.wait_ge(P.sem(f"ld{cb}"), LD_PER_CHUNK * (c // 2 + 1))
                for h in range(32):
                    gh = base + h
                    if c > 0:
                        v.wait_ge(P.sem("pb1"), (c - 1) * 32 + h + 1)
                    if h == 0 and c > 0:
                        v.wait_ge(P.sem("chqst"), 16 * c)
                    v.wait_ge(P.sem("paa"), gh + 1)
                    if gh >= 2:
                        v.wait_ge(P.sem("aexp"), gh - 1)
                    for kt in range(2):
                        ins = v.tensor_scalar(diff_sb[h % 2][:, kt * 256:(kt + 1) * 256],
                                              ps_a[h % 2][:, :],
                                              nacsT[:, kt * 32 + h:kt * 32 + h + 1], 0.0,
                                              ALU.add, ALU.min)
                    P.bump(ins, "dcl", 1)
                    v.wait_ge(P.sem("pae"), gh + 1)
                    P.bump(v.tensor_tensor(chq_sb[:, h * 256:(h + 1) * 256], Cfm[cb][:, :],
                                           ps_e[h % 2][:, :], ALU.mult), "dchq", 1)
                    for kt in range(2):
                        v.tensor_scalar(xd_all[:, h * 256 + kt * 128:h * 256 + (kt + 1) * 128],
                                        xt[kt][:, h * 128:(h + 1) * 128],
                                        dtT[:, kt * 32 + h:kt * 32 + h + 1], None, ALU.mult)
                        v.tensor_scalar(btdd_all[:, h * 128 + kt * 64:h * 128 + (kt + 1) * 64],
                                        bt[cb][kt][:, :],
                                        decT[:, kt * 32 + h:kt * 32 + h + 1], None, ALU.mult)
                    v.wait_ge(P.sem("aexp"), gh + 1)
                    P.bump(v.tensor_tensor(MT_all[:, h * 512:(h + 1) * 512], exp_sb[h % 2][:, :],
                                           CBT[:, :], ALU.mult), "dmt", 1)

            @blk.gpsimd
            def gsa_(g, c=c):
                g.wait_ge(P.sem("dmt"), 32 * (c + 1))
                P.bump(g.dma_start(out=chq[:, c * 32 * 256:(c + 1) * 32 * 256],
                                   in_=chq_sb[:, :]), "chqst", 16)

            @blk.tensor
            def tb_(t, c=c, cb=cb, base=base):
                for h in range(32):
                    gh = base + h
                    t.wait_ge(P.sem("dmt"), gh + 1)
                    if gh >= 2:
                        t.wait_ge(P.sem("dscan"), gh - 1)
                    t.matmul(ps_y[h % 2][:, :], xd_all[:, h * 256:h * 256 + 128],
                             MT_all[:, h * 512:h * 512 + 256], start=True, stop=False)
                    t.matmul(ps_y[h % 2][:, :], xd_all[:, h * 256 + 128:h * 256 + 256],
                             MT_all[:, h * 512 + 256:h * 512 + 512], start=False, stop=False)
                    t.matmul(ps_y[h % 2][:, :], carry[:, h * 128:(h + 1) * 128],
                             chq_sb[:, h * 256:(h + 1) * 256], start=False, stop=False)
                    t.matmul(ps_y[h % 2][:, :], Dident[:, h * 128:(h + 1) * 128],
                             xfm[cb][:, h * 256:(h + 1) * 256], start=False, stop=True)
                    if gh >= 1:
                        t.wait_ge(P.sem("dscan"), gh)
                    t.matmul(ps_s[:, :], btdd_all[:, h * 128:h * 128 + 64],
                             xd_all[:, h * 256:h * 256 + 128], start=True, stop=False)
                    P.bump(t.matmul(ps_s[:, :], btdd_all[:, h * 128 + 64:h * 128 + 128],
                                    xd_all[:, h * 256 + 128:h * 256 + 256],
                                    start=False, stop=True), "pb1", 1)

            @blk.vector
            def vb_(v, c=c, base=base):
                for h in range(32):
                    gh = base + h
                    v.wait_ge(P.sem("pb1"), gh + 1)
                    if h == 0 and c > 0:
                        v.wait_ge(P.sem("yst"), 16 * c)
                    v.tensor_copy(y_sb[:, h * 256:(h + 1) * 256], ps_y[h % 2][:, :])
                    v.tensor_scalar(carry[:, h * 128:(h + 1) * 128],
                                    carry[:, h * 128:(h + 1) * 128],
                                    CD64[:, h:h + 1], None, ALU.mult)
                    P.bump(v.tensor_tensor(carry[:, h * 128:(h + 1) * 128],
                                           carry[:, h * 128:(h + 1) * 128],
                                           ps_s[:, :], ALU.add), "dscan", 1)

            @blk.gpsimd
            def gsb_(g, c=c, c0=c0):
                g.wait_ge(P.sem("dscan"), 32 * (c + 1))
                ins = g.dma_start(
                    out=ypart[:, c0:c0 + 256].rearrange("(pc p) t -> p pc t", p=128),
                    in_=y_sb[:, :].rearrange("p (pc t) -> p pc t", pc=32))
                P.bump(ins, "yst", 16)

        @blk.gpsimd
        def gf_(g):
            P.wait(g, "dscan")
            P.bump(g.dma_start(out=carry_out[:, :], in_=carry[:, :]), "cst_out", 16)
            P.wait(g, "cst_out")
            P.wait(g, "yst")
            P.wait(g, "chqst")
            P.wait(g, "lgst")
    return P


def run_phase_p1(prep, xcb_arrs, dtf_arrs, fused=False):
    P = Prog(fused)
    xcb = P.inp("xcb", xcb_arrs, BF16)
    dtf = P.inp("dtf", dtf_arrs, F32)
    A_log = P.inp("A_log", prep["A_log"])
    D_row = P.inp("D_row", prep["D_row"])
    ypart = P.outp("ypart", (DIN, T), BF16)
    chq = P.outp("chq", (64, NCHK * 32 * 256), BF16)
    carry_out = P.outp("carry_out", (64, 4096), BF16)
    lgrow = P.outp("lgrow", (NCHK, 32), F32)
    build_p1(P, xcb, dtf, A_log, D_row, ypart, chq, carry_out, lgrow)
    return P.run()



def build_p2a(P, ypart, chq, zb, lgrow, carry_in, cmask, xg):
    nc = P.nc
    carry_m = P.sb("a_carrym", [64, 4096], BF16)
    cmask_sb = P.sb("a_cmask", [64, 1], F32)
    lgr = P.sb("a_lgr", [1, NCHK * 32], F32)
    elgr = P.sb("a_elgr", [1, 32], F32)
    ELG64 = P.sb("a_elg64", [64, 32], F32)
    ones1f = P.sb("a_o1f", [1, 64], F32)
    ones128b = P.sb("a_o128b", [128, 128], BF16)
    ctmp = [P.sb(f"a_ctmp{j}", [64, 128], BF16) for j in range(2)]
    y_sb = P.sb("a_y", [128, 32 * 256], BF16)
    chq_sb = P.sb("a_chq", [64, 32 * 256], BF16)
    z_sb = P.sb("a_z", [128, 32 * 256], BF16)
    yf_sb = P.sb("a_yf", [128, 32 * 256], BF16)
    ysq_sb = P.sb("a_ysq", [128, 32 * 256], BF16)
    siluz = [P.sb(f"a_siluz{j}", [128, 256], BF16) for j in range(2)]
    rstd_sb = P.sb("a_rstd", [128, 256], F32)
    tln = P.sb("a_tln", [128, 256], F32)
    epsc = P.sb("a_eps", [128, 1], F32)
    xg_sb = P.sb("a_xg", [128, 32 * 256], BF16)
    ps_y = [P.ps(f"a_psy{j}", [128, 256], F32) for j in range(2)]
    ps_st = P.ps("a_psst", [128, 256], F32)
    ps_e = P.ps("a_pse", [64, 32], F32)
    LD = 3 * 16

    def ystride(hbm, c0):
        return hbm[:, c0:c0 + 256].rearrange("(pc p) t -> p pc t", p=128)

    with nc.Block() as blk:
        @blk.gpsimd
        def g0_(g):
            P.bump(g.dma_start(out=carry_m[:, :], in_=carry_in[:, :]), "ildg", 16)
            P.bump(g.dma_start(out=cmask_sb[:, :], in_=cmask[:, :]), "ildg", 16)
            for c in range(NCHK):
                P.bump(g.dma_start(out=lgr[:, c * 32:(c + 1) * 32], in_=lgrow[c:c + 1, :]), "ildg", 16)
            g.memset(ones1f[:, :], 1.0)
            g.memset(epsc[:, :], EPS_RMS)
            P.bump(g.memset(ones128b[:, :], 1.0), "icst", 1)

        @blk.vector
        def v0_(v):
            P.wait(v, "ildg")
            P.wait(v, "icst")
            P.bump(v.tensor_scalar(carry_m[:, :], carry_m[:, :], cmask_sb[:, :], None,
                                   ALU.mult), "cmsk", 1)

        for c in range(NCHK):
            c0 = c * 256
            base = c * 32

            @blk.sync
            def ld_(s, c=c, c0=c0):
                if c >= 1:
                    s.wait_ge(P.sem("d2y"), 32 * c)
                    s.wait_ge(P.sem("p2mm"), 32 * c)
                    s.wait_ge(P.sem("asil"), 32 * c)
                nld = f"ld{c % 2}"
                P.bump(s.dma_start(out=y_sb[:, :].rearrange("p (pc t) -> p pc t", pc=32),
                                   in_=ystride(ypart, c0)), nld, 16)
                P.bump(s.dma_start(out=chq_sb[:, :], in_=chq[:, c * 8192:(c + 1) * 8192]), nld, 16)
                P.bump(s.dma_start(out=z_sb[:, :].rearrange("p (pc t) -> p pc t", pc=32),
                                   in_=ystride(zb, c0)), nld, 16)

            @blk.scalar
            def a0_(sc, c=c):
                P.wait(sc, "cmsk")
                if c > 0:
                    sc.wait_ge(P.sem("pelg"), c)
                P.bump(sc.activation(elgr[:, :], lgr[:, c * 32:(c + 1) * 32], AF.Exp), "aelg", 1)

            @blk.tensor
            def t0_(t, c=c):
                P.wait(t, "aelg")
                P.bump(t.matmul(ps_e[:, :], ones1f[:, :], elgr[:, :], start=True, stop=True),
                       "pelgmm", 1)

            @blk.scalar
            def a1_(sc, c=c):
                P.wait(sc, "pelgmm")
                P.bump(sc.copy(ELG64[:, :], ps_e[:, :]), "pelg", 1)

            @blk.vector
            def vh_(v, c=c, base=base):
                v.wait_ge(P.sem(f"ld{c % 2}"), LD * (c // 2 + 1))
                P.wait(v, "pelg")
                for h in range(32):
                    gh = base + h
                    if gh >= 2:
                        v.wait_ge(P.sem("p2mm"), gh - 1)
                    P.bump(v.tensor_scalar(ctmp[h % 2][:, :], carry_m[:, h * 128:(h + 1) * 128],
                                           ELG64[:, h:h + 1], None, ALU.mult), "dctm", 1)
                    v.wait_ge(P.sem("p2mm"), gh + 1)
                    P.bump(v.tensor_tensor(yf_sb[:, h * 256:(h + 1) * 256],
                                           y_sb[:, h * 256:(h + 1) * 256],
                                           ps_y[h % 2][:, :], ALU.add), "d2y", 1)

            @blk.tensor
            def th_(t, c=c, base=base):
                for h in range(32):
                    gh = base + h
                    t.wait_ge(P.sem("dctm"), gh + 1)
                    if gh >= 2:
                        t.wait_ge(P.sem("d2y"), gh - 1)
                    P.bump(t.matmul(ps_y[h % 2][:, :], ctmp[h % 2][:, :],
                                    chq_sb[:, h * 256:(h + 1) * 256], start=True, stop=True),
                           "p2mm", 1)

            @blk.scalar
            def asq_(sc, c=c, base=base):
                for h in range(32):
                    gh = base + h
                    sc.wait_ge(P.sem("d2y"), gh + 1)
                    if h == 0 and c > 0:
                        sc.wait_ge(P.sem("smm"), c)
                    P.bump(sc.activation(ysq_sb[:, h * 256:(h + 1) * 256],
                                         yf_sb[:, h * 256:(h + 1) * 256], AF.Square), "asq", 1)

            @blk.tensor
            def tst_(t, c=c):
                t.wait_ge(P.sem("asq"), 32 * (c + 1))
                if c > 0:
                    t.wait_ge(P.sem("arstd"), c)
                for h in range(32):
                    ins = t.matmul(ps_st[:, :], ones128b[:, :], ysq_sb[:, h * 256:(h + 1) * 256],
                                   start=(h == 0), stop=(h == 31))
                P.bump(ins, "smm", 1)

            @blk.scalar
            def arst_(sc, c=c):
                P.wait(sc, "smm")
                if c > 0:
                    sc.wait_ge(P.sem("dxg"), 32 * c)
                sc.activation(tln[:, :], ps_st[:, :], AF.Ln, bias=epsc[:, :], scale=1.0 / DIN)
                P.bump(sc.activation(rstd_sb[:, :], tln[:, :], AF.Exp, scale=-0.5), "arstd", 1)

            @blk.scalar
            def asil_(sc, c=c, base=base):
                for h in range(32):
                    gh = base + h
                    if gh >= 2:
                        sc.wait_ge(P.sem("dxg"), gh - 1)
                    P.bump(sc.activation(siluz[h % 2][:, :], z_sb[:, h * 256:(h + 1) * 256],
                                         AF.Silu), "asil", 1)

            @blk.vector
            def vg_(v, c=c, base=base):
                P.wait(v, "arstd")
                for h in range(32):
                    gh = base + h
                    if h == 0 and c > 0:
                        v.wait_ge(P.sem("xgst"), 16 * c)
                    v.wait_ge(P.sem("asil"), gh + 1)
                    v.tensor_tensor(xg_sb[:, h * 256:(h + 1) * 256],
                                    yf_sb[:, h * 256:(h + 1) * 256], rstd_sb[:, :], ALU.mult)
                    P.bump(v.tensor_tensor(xg_sb[:, h * 256:(h + 1) * 256],
                                           xg_sb[:, h * 256:(h + 1) * 256],
                                           siluz[h % 2][:, :], ALU.mult), "dxg", 1)

            @blk.gpsimd
            def gs_(g, c=c, c0=c0):
                g.wait_ge(P.sem("dxg"), 32 * (c + 1))
                P.bump(g.dma_start(out=ystride(xg, c0),
                                   in_=xg_sb[:, :].rearrange("p (pc t) -> p pc t", pc=32)),
                       "xgst", 16)

        @blk.gpsimd
        def gf_(g):
            P.wait(g, "xgst")
    return P


def run_phase_p2a(prep, ypart_arrs, chq_arrs, zb_arrs, lgrow_arrs, carry_arrs, fused=False):
    P = Prog(fused)
    ypart = P.inp("ypart", ypart_arrs, BF16)
    chq = P.inp("chq", chq_arrs, BF16)
    zb = P.inp("zb", zb_arrs, BF16)
    lgrow = P.inp("lgrow", lgrow_arrs, F32)
    carry_in = P.inp("carry_in", carry_arrs, BF16)
    cmask = P.inp("cmask", prep["carry_mask"])
    xg = P.outp("xg", (DIN, T), BF16)
    build_p2a(P, ypart, chq, zb, lgrow, carry_in, cmask, xg)
    return P.run()



def build_p2b(P, xg, xT, w_out_p, x1, tg=1024, src_off=3, double_res=False,
              name="b"):
    nc = P.nc
    NG = T // tg
    KCH = w_out_p.shape[2] // 128
    xg_sb = P.sb(name + "_xg", [128, KCH * tg], BF16)
    wbuf = [P.sb(f"{name}_w{j}", [128, KCH * 128], BF16) for j in range(2)]
    xres = [P.sb(f"{name}_xr{j}", [128, 512], F32) for j in range(2)]
    s_sb = P.sb(name + "_s", [128, 16 * tg], F32)
    sbf = [P.sb(f"{name}_sbf{j}", [128, 512], BF16) for j in range(4)]
    ssq = [P.sb(f"{name}_ssq{j}", [128, 512], BF16) for j in range(4)]
    ones128b = P.sb(name + "_o128", [128, 128], BF16)
    epsc = P.sb(name + "_eps", [128, 1], F32)
    mu_sb = P.sb(name + "_mu", [128, tg], F32)
    var_sb = P.sb(name + "_var", [128, tg], F32)
    rstd_sb = P.sb(name + "_rstd", [128, tg], F32)
    x1t = [P.sb(f"{name}_x1t{j}", [128, 512], F32) for j in range(2)]
    pm = [P.ps(f"{name}_pm{j}", [128, 512], F32) for j in range(2)]
    ps1 = P.ps(name + "_ps1", [128, tg], F32)
    ps2 = P.ps(name + "_ps2", [128, tg], F32)
    NT = tg // 512

    with nc.Block() as blk:
        @blk.gpsimd
        def g0_(g):
            g.memset(epsc[:, :], EPS_LN)
            P.bump(g.memset(ones128b[:, :], 1.0), "icst", 1)
            for gi in range(NG):
                for mb in range(16):
                    k = gi * 16 + mb
                    if k >= 2:
                        g.wait_ge(P.sem("sres"), NT * (k - 1))
                    P.bump(g.dma_start(out=wbuf[k % 2][:, :], in_=w_out_p[mb]),
                           f"wld{k % 2}", 16)

        @blk.sync
        def s0_(s):
            for gi in range(NG):
                if gi > 0:
                    s.wait_ge(P.sem("sres"), 16 * NT * gi)
                P.bump(s.dma_start(
                    out=xg_sb[:, :].rearrange("p (kc t) -> p kc t", kc=KCH),
                    in_=xg[:, gi * tg:(gi + 1) * tg].rearrange("(kc p) t -> p kc t", p=128)),
                    f"xgld{gi % 2}", 16)
                for mb in range(16):
                    for tt in range(NT):
                        j = (gi * 16 + mb) * NT + tt
                        if j >= 2:
                            s.wait_ge(P.sem("sres"), j - 1)
                        P.bump(s.dma_start(
                            out=xres[j % 2][:, :],
                            in_=xT[mb * 128:(mb + 1) * 128,
                                   src_off + gi * tg + tt * 512:src_off + gi * tg + tt * 512 + 512]),
                            f"xrld{j % 2}", 16)
                for mb in range(16):
                    for tt in range(NT):
                        j2 = gi * 16 * NT + mb * NT + tt
                        s.wait_ge(P.sem("x1d"), j2 + 1)
                        P.bump(s.dma_start(
                            out=x1[mb * 128:(mb + 1) * 128,
                                   gi * tg + tt * 512:gi * tg + tt * 512 + 512],
                            in_=x1t[j2 % 2][:, :]), "x1st", 16)

        @blk.tensor
        def t0_(t):
            P.wait(t, "icst")
            for gi in range(NG):
                t.wait_ge(P.sem(f"xgld{gi % 2}"), 16 * (gi // 2 + 1))
                if gi > 0:
                    t.wait_ge(P.sem("aln"), gi)

                def stats_mm(jl, gi=gi):
                    j = gi * 16 * NT + jl
                    mb_s, tt_s = jl // NT, jl % NT
                    t.wait_ge(P.sem("ssqa"), j + 1)
                    t.matmul(ps1[:, tt_s * 512:(tt_s + 1) * 512], ones128b[:, :],
                             sbf[j % 4][:, :], start=(mb_s == 0), stop=(mb_s == 15))
                    ins = t.matmul(ps2[:, tt_s * 512:(tt_s + 1) * 512], ones128b[:, :],
                                   ssq[j % 4][:, :], start=(mb_s == 0), stop=(mb_s == 15))
                    P.bump(ins, "stmm", 2)

                for mb in range(16):
                    k = gi * 16 + mb
                    t.wait_ge(P.sem(f"wld{k % 2}"), 16 * (k // 2 + 1))
                    for tt in range(NT):
                        jl = mb * NT + tt
                        j = k * NT + tt
                        if j >= 2:
                            t.wait_ge(P.sem("sres"), j - 1)
                        for kc in range(KCH):
                            ins = t.matmul(pm[j % 2][:, :],
                                           wbuf[k % 2][:, kc * 128:(kc + 1) * 128],
                                           xg_sb[:, kc * tg + tt * 512:kc * tg + tt * 512 + 512],
                                           start=(kc == 0), stop=(kc == KCH - 1))
                        P.bump(ins, "mm", 1)
                        if jl >= 2:
                            stats_mm(jl - 2)
                stats_mm(16 * NT - 2)
                stats_mm(16 * NT - 1)

        @blk.vector
        def v0_(v):
            for gi in range(NG):
                for mb in range(16):
                    for tt in range(NT):
                        j = (gi * 16 + mb) * NT + tt
                        v.wait_ge(P.sem("mm"), j + 1)
                        v.wait_ge(P.sem(f"xrld{j % 2}"), 16 * (j // 2 + 1))
                        col = mb * tg + tt * 512
                        if double_res:
                            ins = v.scalar_tensor_tensor(s_sb[:, col:col + 512],
                                                         xres[j % 2][:, :], 2.0,
                                                         pm[j % 2][:, :], ALU.mult, ALU.add)
                        else:
                            ins = v.tensor_tensor(s_sb[:, col:col + 512], pm[j % 2][:, :],
                                                  xres[j % 2][:, :], ALU.add)
                        P.bump(ins, "sres", 1)
                        if j >= 4:
                            v.wait_ge(P.sem("stmm"), 2 * (j - 3))
                        P.bump(v.tensor_copy(sbf[j % 4][:, :], s_sb[:, col:col + 512]),
                               "sbfc", 1)
                v.wait_ge(P.sem("aln0"), gi + 1)
                P.bump(v.tensor_tensor(var_sb[:, :], var_sb[:, :], rstd_sb[:, :],
                                       ALU.subtract), "vvar", 1)
                v.wait_ge(P.sem("aln"), gi + 1)
                for mb in range(16):
                    for tt in range(NT):
                        j2 = gi * 16 * NT + mb * NT + tt
                        col = mb * tg + tt * 512
                        if j2 >= 2:
                            v.wait_ge(P.sem("x1st"), 16 * (j2 - 1))
                        v.tensor_tensor(x1t[j2 % 2][:, :], s_sb[:, col:col + 512],
                                        mu_sb[:, tt * 512:(tt + 1) * 512], ALU.subtract)
                        P.bump(v.tensor_tensor(x1t[j2 % 2][:, :], x1t[j2 % 2][:, :],
                                               rstd_sb[:, tt * 512:(tt + 1) * 512], ALU.mult),
                               "x1d", 1)

        @blk.scalar
        def a0_(sc):
            for gi in range(NG):
                for mb in range(16):
                    for tt in range(NT):
                        j = (gi * 16 + mb) * NT + tt
                        sc.wait_ge(P.sem("sbfc"), j + 1)
                        P.bump(sc.activation(ssq[j % 4][:, :], sbf[j % 4][:, :], AF.Square),
                               "ssqa", 1)
                sc.wait_ge(P.sem("stmm"), 2 * 16 * NT * (gi + 1))
                if gi > 0:
                    sc.wait_ge(P.sem("x1d"), 16 * NT * gi)
                sc.mul(mu_sb[:, :], ps1[:, :], 1.0 / D)
                sc.mul(var_sb[:, :], ps2[:, :], 1.0 / D)
                P.bump(sc.activation(rstd_sb[:, :], mu_sb[:, :], AF.Square), "aln0", 1)
                sc.wait_ge(P.sem("vvar"), gi + 1)
                sc.activation(var_sb[:, :], var_sb[:, :], AF.Ln, bias=epsc[:, :])
                P.bump(sc.activation(rstd_sb[:, :], var_sb[:, :], AF.Exp, scale=-0.5),
                       "aln", 1)

        @blk.gpsimd
        def g1_(g):
            P.wait(g, "x1st")
    return P


def run_phase_p2b(prep, xg_arrs, fused=False):
    P = Prog(fused)
    xg = P.inp("xg", xg_arrs, BF16)
    xT = P.inp("xT", prep["xT"])
    w_out_p = P.inp("w_out_p", prep["w_out_p"])
    x1 = P.outp("x1", (D, T), F32)
    build_p2b(P, xg, xT, w_out_p, x1)
    return P.run()



def build_p3a(P, x1, w_router, sA_gate, sA_up, A_gate_all, A_up_all,
              sB_gate, sB_up, B_gate_all, B_up_all, ebias_row,
              w_gate_p, w_up_p, hid, W64b):
    nc = P.nc
    TG = 512
    NG = T // TG
    xn = P.sb("fa_xn", [128, 16 * TG], BF16)
    wbg = [P.sb(f"fa_wg{j}", [128, 2048], BF16) for j in range(2)]
    wbu = [P.sb(f"fa_wu{j}", [128, 2048], BF16) for j in range(2)]
    router_sb = P.sb("fa_rt", [128, 16 * 8], BF16)
    sAg_sb = P.sb("fa_sag", [128, 16 * 8], BF16)
    sAu_sb = P.sb("fa_sau", [128, 16 * 8], BF16)
    Ag_sb = P.sb("fa_ag", [128, 16 * 64], BF16)
    Au_sb = P.sb("fa_au", [128, 16 * 64], BF16)
    sBg_sb = P.sb("fa_sbg", [8, DFF], BF16)
    sBu_sb = P.sb("fa_sbu", [8, DFF], BF16)
    Bg_sb = P.sb("fa_bg", [64, DFF], BF16)
    Bu_sb = P.sb("fa_bu", [64, DFF], BF16)
    ebias_sb = P.sb("fa_eb", [1, 8], F32)
    eb128 = P.sb("fa_eb128", [128, 8], F32)
    ones1f = P.sb("fa_o1f", [1, 128], F32)
    iotar = P.sb("fa_iotar", [8, 64], F32)
    REPsc = P.sb("fa_rep", [8, 64], BF16)
    idf = P.sb("fa_idf", [128, 128], F32)
    iot = P.sb("fa_iot", [128, 128], F32)
    scores_fm = P.sb("fa_scf", [8, TG], F32)
    s_tok = P.sb("fa_stok", [128, 32], F32)
    biased = P.sb("fa_bia", [128, 8], F32)
    m1 = P.sb("fa_m1", [128, 1], F32)
    m2 = P.sb("fa_m2", [128, 1], F32)
    eq1 = P.sb("fa_eq1", [128, 8], F32)
    b2 = P.sb("fa_b2", [128, 8], F32)
    msk = P.sb("fa_msk", [128, 8], F32)
    topk = P.sb("fa_topk", [128, 8], F32)
    den = P.sb("fa_den", [128, 1], F32)
    w_tokb = [P.sb(f"fa_wtok{j}", [128, 8], F32) for j in range(2)]
    w_fm = P.sb("fa_wfm", [8, TG], BF16)
    W64_sb = P.sb("fa_w64", [64, TG], BF16)
    sxa_g = P.sb("fa_sxag", [8, TG], BF16)
    sxa_u = P.sb("fa_sxau", [8, TG], BF16)
    exw_g = P.sb("fa_exwg", [64, TG], BF16)
    exw_u = P.sb("fa_exwu", [64, TG], BF16)
    sg = P.sb("fa_sg", [128, 512], BF16)
    hid_sb = [P.sb(f"fa_hid{j}", [128, 512], BF16) for j in range(4)]
    ps_sm = P.ps("fa_pssm", [128, 512], F32)
    ps_g = [P.ps(f"fa_psg{j}", [128, 512], F32) for j in range(2)]
    ps_u = [P.ps(f"fa_psu{j}", [128, 512], F32) for j in range(2)]

    def vse(v, ins):
        n = P.bump(ins, "vself", 1)
        v.wait_ge(P.sem("vself"), n)

    with nc.Block() as blk:
        @blk.gpsimd
        def g0_(g):
            P.bump(g.dma_start(out=router_sb[:, :], in_=w_router[:, :]), "ildg", 16)
            P.bump(g.dma_start(out=sAg_sb[:, :], in_=sA_gate[:, :]), "ildg", 16)
            P.bump(g.dma_start(out=sAu_sb[:, :], in_=sA_up[:, :]), "ildg", 16)
            P.bump(g.dma_start(out=Ag_sb[:, :], in_=A_gate_all[:, :]), "ildg", 16)
            P.bump(g.dma_start(out=Au_sb[:, :], in_=A_up_all[:, :]), "ildg", 16)
            P.bump(g.dma_start(out=sBg_sb[:, :], in_=sB_gate[:, :]), "ildg", 16)
            P.bump(g.dma_start(out=sBu_sb[:, :], in_=sB_up[:, :]), "ildg", 16)
            P.bump(g.dma_start(out=Bg_sb[:, :], in_=B_gate_all[:, :]), "ildg", 16)
            P.bump(g.dma_start(out=Bu_sb[:, :], in_=B_up_all[:, :]), "ildg", 16)
            P.bump(g.dma_start(out=ebias_sb[:, :], in_=ebias_row[:, :]), "ildg", 16)
            g.iota(iot[:, :], [[1, 128]], channel_multiplier=-1,
                   allow_small_or_imprecise_dtypes=True)
            g.iota(iotar[:, :], [[1, 8], [0, 8]], channel_multiplier=-1,
                   allow_small_or_imprecise_dtypes=True)
            g.memset(ones1f[:, :], 1.0)
            P.bump(g.nop(), "icst", 1)
            for gi in range(NG):
                if gi > 0:
                    g.wait_ge(P.sem("humul"), FFB * gi)
                for kc in range(16):
                    P.bump(g.dma_start(out=xn[:, kc * TG:(kc + 1) * TG],
                                       in_=x1[kc * 128:(kc + 1) * 128, gi * TG:(gi + 1) * TG]),
                           f"xnld{gi % 2}", 16)
                for mb in range(FFB):
                    k = gi * FFB + mb
                    if k >= 2:
                        g.wait_ge(P.sem("humul"), k - 1)
                    P.bump(g.dma_start(out=wbg[k % 2][:, :], in_=w_gate_p[mb]), f"wgld{k % 2}", 16)
                    P.bump(g.dma_start(out=wbu[k % 2][:, :], in_=w_up_p[mb]), f"wuld{k % 2}", 16)

        @blk.vector
        def v0_(v):
            P.wait(v, "icst")
            P.wait(v, "ildg")
            v.tensor_scalar(idf[:, :], iot[:, :], 0.0, None, ALU.is_equal)
            P.bump(v.tensor_scalar(REPsc[:, :], iotar[:, :], 0.0, None, ALU.is_equal), "repa", 1)
            v.wait_ge(P.sem("repa"), 1)
            P.bump(v.tensor_scalar(REPsc[:, :], REPsc[:, :], 0.125, None, ALU.mult), "cst1", 1)

        for gi in range(NG):
            @blk.tensor
            def tr_(t, gi=gi):
                P.wait(t, "cst1")
                t.wait_ge(P.sem(f"xnld{gi % 2}"), 16 * 16 * (gi // 2 + 1))
                if gi > 0:
                    t.wait_ge(P.sem("rdone"), gi)
                for kc in range(16):
                    ins = t.matmul(ps_sm[0:8, 0:TG], router_sb[:, kc * 8:(kc + 1) * 8],
                                   xn[:, kc * TG:(kc + 1) * TG], start=(kc == 0), stop=(kc == 15))
                P.bump(ins, "rmm", 1)

            @blk.scalar
            def ar_(sc, gi=gi):
                P.wait(sc, "rmm")
                P.bump(sc.activation(scores_fm[:, :], ps_sm[0:8, 0:TG], AF.Sigmoid), "rsig", 1)

            @blk.tensor
            def tr2_(t, gi=gi):
                P.wait(t, "rsig")
                if gi == 0:
                    t.matmul(ps_sm[:, 448:456], ones1f[:, :], ebias_sb[:, :], start=True, stop=True)
                for t2 in range(4):
                    ins = t.transpose(ps_sm[:, 256 + t2 * 8:256 + (t2 + 1) * 8],
                                      scores_fm[:, t2 * 128:(t2 + 1) * 128], idf[0:8, 0:8])
                P.bump(ins, "rtr", 1)

            @blk.vector
            def vr_(v, gi=gi):
                P.wait(v, "rtr")
                if gi == 0:
                    v.tensor_copy(eb128[:, :], ps_sm[:, 448:456])
                v.tensor_copy(s_tok[:, :], ps_sm[:, 256:288])
                for t2 in range(4):
                    jt = gi * 4 + t2
                    st = s_tok[:, t2 * 8:(t2 + 1) * 8]
                    vse(v, v.tensor_tensor(biased[:, :], st, eb128[:, :], ALU.add))
                    vse(v, v.reduce_max(m1[:, :], biased[:, :], axis=AX.X))
                    vse(v, v.tensor_scalar(eq1[:, :], biased[:, :], m1[:, :], None, ALU.is_equal))
                    vse(v, v.scalar_tensor_tensor(b2[:, :], eq1[:, :], -1e30, biased[:, :],
                                                  ALU.mult, ALU.add))
                    vse(v, v.reduce_max(m2[:, :], b2[:, :], axis=AX.X))
                    vse(v, v.tensor_scalar(msk[:, :], biased[:, :], m2[:, :], None, ALU.is_ge))
                    vse(v, v.tensor_tensor(topk[:, :], st, msk[:, :], ALU.mult))
                    vse(v, v.reduce_sum(den[:, :], topk[:, :], axis=AX.X))
                    vse(v, v.tensor_scalar(den[:, :], den[:, :], 1e-8, None, ALU.add))
                    vse(v, v.reciprocal(den[:, :], den[:, :]))
                    if jt >= 2:
                        v.wait_ge(P.sem("wtr"), jt - 1)
                    ins = v.tensor_scalar(w_tokb[t2 % 2][:, :], topk[:, :], den[:, :],
                                          None, ALU.mult)
                    P.bump(ins, "wtok", 1)

            @blk.tensor
            def tr3_(t, gi=gi):
                for t2 in range(4):
                    jt = gi * 4 + t2
                    t.wait_ge(P.sem("wtok"), jt + 1)
                    ins = t.transpose(ps_sm[0:8, t2 * 128:(t2 + 1) * 128],
                                      w_tokb[t2 % 2][:, :], idf[:, :])
                    P.bump(ins, "wtr", 1)

            @blk.vector
            def vr2_(v, gi=gi):
                v.wait_ge(P.sem("wtr"), 4 * (gi + 1))
                P.bump(v.tensor_copy(w_fm[:, :], ps_sm[0:8, 0:TG]), "wfmd", 1)

            @blk.tensor
            def tr4_(t, gi=gi):
                P.wait(t, "wfmd")
                P.bump(t.matmul(ps_sm[0:64, 0:TG], REPsc[:, :], w_fm[:, :],
                                start=True, stop=True), "w64mm", 1)

            @blk.vector
            def vr3_(v, gi=gi):
                P.wait(v, "w64mm")
                if gi > 0:
                    v.wait_ge(P.sem("w64st"), 16 * gi)
                P.bump(v.tensor_copy(W64_sb[:, :], ps_sm[0:64, 0:TG]), "w64d", 1)

            @blk.sync
            def gw64_(s, gi=gi):
                P.wait(s, "w64d")
                P.bump(s.dma_start(out=W64b[:, gi * TG:(gi + 1) * TG], in_=W64_sb[:, :]),
                       "w64st", 16)

            @blk.tensor
            def tl_(t, gi=gi):
                P.wait(t, "w64d")
                if gi > 0:
                    t.wait_ge(P.sem("humul"), FFB * gi)
                for kc in range(16):
                    ins = t.matmul(ps_sm[0:8, 0:TG], sAg_sb[:, kc * 8:(kc + 1) * 8],
                                   xn[:, kc * TG:(kc + 1) * TG], start=(kc == 0), stop=(kc == 15))
                P.bump(ins, "sxag", 1)
                for kc in range(16):
                    ins = t.matmul(ps_u[0][0:8, 0:TG], sAu_sb[:, kc * 8:(kc + 1) * 8],
                                   xn[:, kc * TG:(kc + 1) * TG], start=(kc == 0), stop=(kc == 15))
                P.bump(ins, "sxau", 1)
                if gi > 0:
                    t.wait_ge(P.sem("humul"), FFB * gi)
                for kc in range(16):
                    ins = t.matmul(ps_g[0][0:64, 0:TG], Ag_sb[:, kc * 64:(kc + 1) * 64],
                                   xn[:, kc * TG:(kc + 1) * TG], start=(kc == 0), stop=(kc == 15))
                P.bump(ins, "exag", 1)
                for kc in range(16):
                    ins = t.matmul(ps_g[1][0:64, 0:TG], Au_sb[:, kc * 64:(kc + 1) * 64],
                                   xn[:, kc * TG:(kc + 1) * TG], start=(kc == 0), stop=(kc == 15))
                P.bump(ins, "exau", 1)

            @blk.scalar
            def al_(sc, gi=gi):
                P.wait(sc, "sxag")
                P.bump(sc.mul(sxa_g[:, :], ps_sm[0:8, 0:TG], 0.125), "sxagd", 1)
                P.wait(sc, "sxau")
                P.bump(sc.mul(sxa_u[:, :], ps_u[0][0:8, 0:TG], 0.125), "sxaud", 1)

            @blk.vector
            def vl_(v, gi=gi):
                P.wait(v, "exag")
                P.bump(v.tensor_tensor(exw_g[:, :], ps_g[0][0:64, 0:TG], W64_sb[:, :],
                                       ALU.mult), "exgd", 1)
                P.wait(v, "exau")
                P.bump(v.tensor_tensor(exw_u[:, :], ps_g[1][0:64, 0:TG], W64_sb[:, :],
                                       ALU.mult), "exud", 1)

            @blk.tensor
            def tm_(t, gi=gi):
                P.wait(t, "sxagd")
                P.wait(t, "sxaud")
                P.wait(t, "exgd")
                P.wait(t, "exud")
                for mb in range(FFB):
                    k = gi * FFB + mb
                    t.wait_ge(P.sem(f"wgld{k % 2}"), 16 * (k // 2 + 1))
                    t.wait_ge(P.sem(f"wuld{k % 2}"), 16 * (k // 2 + 1))
                    if k >= 2:
                        t.wait_ge(P.sem("humul"), k - 1)
                    for kc in range(16):
                        t.matmul(ps_g[k % 2][:, :], wbg[k % 2][:, kc * 128:(kc + 1) * 128],
                                 xn[:, kc * TG:(kc + 1) * TG], start=(kc == 0), stop=False)
                    t.matmul(ps_g[k % 2][:, :], sBg_sb[:, mb * 128:(mb + 1) * 128],
                             sxa_g[:, :], start=False, stop=False)
                    ins = t.matmul(ps_g[k % 2][:, :], Bg_sb[:, mb * 128:(mb + 1) * 128],
                                   exw_g[:, :], start=False, stop=True)
                    P.bump(ins, "gmm", 1)
                    for kc in range(16):
                        t.matmul(ps_u[k % 2][:, :], wbu[k % 2][:, kc * 128:(kc + 1) * 128],
                                 xn[:, kc * TG:(kc + 1) * TG], start=(kc == 0), stop=False)
                    t.matmul(ps_u[k % 2][:, :], sBu_sb[:, mb * 128:(mb + 1) * 128],
                             sxa_u[:, :], start=False, stop=False)
                    ins = t.matmul(ps_u[k % 2][:, :], Bu_sb[:, mb * 128:(mb + 1) * 128],
                                   exw_u[:, :], start=False, stop=True)
                    P.bump(ins, "umm", 1)

            @blk.scalar
            def am_(sc, gi=gi):
                for mb in range(FFB):
                    k = gi * FFB + mb
                    sc.wait_ge(P.sem("gmm"), k + 1)
                    P.bump(sc.activation(sg[:, :], ps_g[k % 2][:, :], AF.Silu), "hsil", 1)

            @blk.vector
            def vm_(v, gi=gi):
                for mb in range(FFB):
                    k = gi * FFB + mb
                    v.wait_ge(P.sem("hsil"), k + 1)
                    v.wait_ge(P.sem("umm"), k + 1)
                    if k >= 4:
                        v.wait_ge(P.sem("hst"), 16 * (k - 3))
                    P.bump(v.tensor_tensor(hid_sb[k % 4][:, :], sg[:, :], ps_u[k % 2][:, :],
                                           ALU.mult), "humul", 1)

            @blk.sync
            def sm_(s, gi=gi):
                for mb in range(FFB):
                    k = gi * FFB + mb
                    s.wait_ge(P.sem("humul"), k + 1)
                    P.bump(s.dma_start(out=hid[mb * 128:(mb + 1) * 128, gi * TG:(gi + 1) * TG],
                                       in_=hid_sb[k % 4][:, :]), "hst", 16)

            @blk.tensor
            def trd_(t, gi=gi):
                P.wait(t, "sxagd")
                P.bump(t.nop(), "rdone", 1)

        @blk.gpsimd
        def gf_(g):
            P.wait(g, "hst")
            P.wait(g, "w64st")
    return P


def run_phase_p3a(prep, x1_arrs, fused=False):
    P = Prog(fused)
    x1 = P.inp("x1", x1_arrs, F32)
    w_router = P.inp("l0_w_router", prep["l0_w_router"])
    sA_gate = P.inp("l0_sA_gate", prep["l0_sA_gate"])
    sA_up = P.inp("l0_sA_up", prep["l0_sA_up"])
    A_gate_all = P.inp("l0_A_gate_all", prep["l0_A_gate_all"])
    A_up_all = P.inp("l0_A_up_all", prep["l0_A_up_all"])
    sB_gate = P.inp("l0_sB_gate", prep["l0_sB_gate"])
    sB_up = P.inp("l0_sB_up", prep["l0_sB_up"])
    B_gate_all = P.inp("l0_B_gate_all", prep["l0_B_gate_all"])
    B_up_all = P.inp("l0_B_up_all", prep["l0_B_up_all"])
    ebias_row = P.inp("l0_ebias_row", prep["l0_ebias_row"])
    w_gate_p = P.inp("l0_w_gate_p", prep["l0_w_gate_p"])
    w_up_p = P.inp("l0_w_up_p", prep["l0_w_up_p"])
    hid = P.outp("hid", (DFF, T), BF16)
    W64b = P.outp("W64b", (64, T), BF16)
    build_p3a(P, x1, w_router, sA_gate, sA_up, A_gate_all, A_up_all,
              sB_gate, sB_up, B_gate_all, B_up_all, ebias_row,
              w_gate_p, w_up_p, hid, W64b)
    return P.run()



def build_p3b(P, hid, W64b, x1, w_down_p, sA_down, A_down_all, sB_down, B_down_all, x2):
    nc = P.nc
    tg = 512
    NG = T // tg
    KCH = FFB
    NT = 1
    xg_sb = P.sb("d_xg", [128, KCH * tg], BF16)
    wbuf = [P.sb(f"d_w{j}", [128, KCH * 128], BF16) for j in range(2)]
    xres = [P.sb(f"d_xr{j}", [128, 512], F32) for j in range(2)]
    s_sb = P.sb("d_s", [128, 16 * tg], F32)
    sbf = [P.sb(f"d_sbf{j}", [128, 512], BF16) for j in range(4)]
    ssq = [P.sb(f"d_ssq{j}", [128, 512], BF16) for j in range(4)]
    ones128b = P.sb("d_o128", [128, 128], BF16)
    epsc = P.sb("d_eps", [128, 1], F32)
    mu_sb = P.sb("d_mu", [128, tg], F32)
    var_sb = P.sb("d_var", [128, tg], F32)
    rstd_sb = P.sb("d_rstd", [128, tg], F32)
    x1t = [P.sb(f"d_x1t{j}", [128, 512], F32) for j in range(2)]
    sAd_sb = P.sb("d_sad", [128, KCH * 8], BF16)
    Ad_sb = P.sb("d_ad", [128, KCH * 64], BF16)
    sBd_sb = P.sb("d_sbd", [8, D], BF16)
    Bd_sb = P.sb("d_bd", [64, D], BF16)
    W64g = P.sb("d_w64", [64, tg], BF16)
    hsxa = P.sb("d_hsxa", [8, tg], BF16)
    hxw = P.sb("d_hxw", [64, tg], BF16)
    pm = [P.ps(f"d_pm{j}", [128, 512], F32) for j in range(2)]
    ps1 = P.ps("d_ps1", [128, tg], F32)
    ps2 = P.ps("d_ps2", [128, tg], F32)
    ps_l = P.ps("d_psl", [128, 512], F32)

    with nc.Block() as blk:
        @blk.gpsimd
        def g0_(g):
            g.memset(epsc[:, :], EPS_LN)
            P.bump(g.dma_start(out=sAd_sb[:, :], in_=sA_down[:, :]), "ildg", 16)
            P.bump(g.dma_start(out=Ad_sb[:, :], in_=A_down_all[:, :]), "ildg", 16)
            P.bump(g.dma_start(out=sBd_sb[:, :], in_=sB_down[:, :]), "ildg", 16)
            P.bump(g.dma_start(out=Bd_sb[:, :], in_=B_down_all[:, :]), "ildg", 16)
            P.bump(g.memset(ones128b[:, :], 1.0), "icst", 1)
            for gi in range(NG):
                for mb in range(16):
                    k = gi * 16 + mb
                    if k >= 2:
                        g.wait_ge(P.sem("sres"), NT * (k - 1))
                    P.bump(g.dma_start(out=wbuf[k % 2][:, :], in_=w_down_p[mb]),
                           f"wld{k % 2}", 16)

        @blk.sync
        def s0_(s):
            for gi in range(NG):
                if gi > 0:
                    s.wait_ge(P.sem("sres"), 16 * NT * gi)
                P.bump(s.dma_start(
                    out=xg_sb[:, :].rearrange("p (kc t) -> p kc t", kc=KCH),
                    in_=hid[:, gi * tg:(gi + 1) * tg].rearrange("(kc p) t -> p kc t", p=128)),
                    f"xgld{gi % 2}", 16)
                P.bump(s.dma_start(out=W64g[:, :], in_=W64b[:, gi * tg:(gi + 1) * tg]),
                       f"xgld{gi % 2}", 16)
                for mb in range(16):
                    j = gi * 16 + mb
                    if j >= 2:
                        s.wait_ge(P.sem("sres"), j - 1)
                    P.bump(s.dma_start(
                        out=xres[j % 2][:, :],
                        in_=x1[mb * 128:(mb + 1) * 128, gi * tg:gi * tg + 512]),
                        f"xrld{j % 2}", 16)
                for mb in range(16):
                    j2 = gi * 16 + mb
                    s.wait_ge(P.sem("x1d"), j2 + 1)
                    P.bump(s.dma_start(
                        out=x2[mb * 128:(mb + 1) * 128, gi * tg:gi * tg + 512],
                        in_=x1t[j2 % 2][:, :]), "x1st", 16)

        @blk.tensor
        def t0_(t):
            P.wait(t, "icst")
            P.wait(t, "ildg")
            for gi in range(NG):
                t.wait_ge(P.sem(f"xgld{gi % 2}"), 32 * (gi // 2 + 1))
                if gi > 0:
                    t.wait_ge(P.sem("aln"), gi)
                    t.wait_ge(P.sem("hxwd"), gi)
                for kc in range(KCH):
                    ins = t.matmul(ps_l[0:8, 0:tg], sAd_sb[:, kc * 8:(kc + 1) * 8],
                                   xg_sb[:, kc * tg:(kc + 1) * tg], start=(kc == 0),
                                   stop=(kc == KCH - 1))
                P.bump(ins, "hsxamm", 1)
                t.wait_ge(P.sem("hsxad"), gi + 1)
                for kc in range(KCH):
                    ins = t.matmul(ps_l[0:64, 0:tg], Ad_sb[:, kc * 64:(kc + 1) * 64],
                                   xg_sb[:, kc * tg:(kc + 1) * tg], start=(kc == 0),
                                   stop=(kc == KCH - 1))
                P.bump(ins, "hxamm", 1)
                t.wait_ge(P.sem("hxwd"), gi + 1)

                def stats_mm(jl, gi=gi):
                    j = gi * 16 + jl
                    t.wait_ge(P.sem("ssqa"), j + 1)
                    t.matmul(ps1[:, 0:512], ones128b[:, :],
                             sbf[j % 4][:, :], start=(jl == 0), stop=(jl == 15))
                    ins = t.matmul(ps2[:, 0:512], ones128b[:, :],
                                   ssq[j % 4][:, :], start=(jl == 0), stop=(jl == 15))
                    P.bump(ins, "stmm", 2)

                for mb in range(16):
                    k = gi * 16 + mb
                    t.wait_ge(P.sem(f"wld{k % 2}"), 16 * (k // 2 + 1))
                    if k >= 2:
                        t.wait_ge(P.sem("sres"), k - 1)
                    for kc in range(KCH):
                        t.matmul(pm[k % 2][:, :], wbuf[k % 2][:, kc * 128:(kc + 1) * 128],
                                 xg_sb[:, kc * tg:(kc + 1) * tg], start=(kc == 0), stop=False)
                    t.matmul(pm[k % 2][:, :], sBd_sb[:, mb * 128:(mb + 1) * 128],
                             hsxa[:, :], start=False, stop=False)
                    ins = t.matmul(pm[k % 2][:, :], Bd_sb[:, mb * 128:(mb + 1) * 128],
                                   hxw[:, :], start=False, stop=True)
                    P.bump(ins, "mm", 1)
                    if mb >= 2:
                        stats_mm(mb - 2)
                stats_mm(14)
                stats_mm(15)

        @blk.scalar
        def al_(sc):
            for gi in range(NG):
                P.wait(sc, "hsxamm", (gi + 1))
                P.bump(sc.mul(hsxa[:, :], ps_l[0:8, 0:tg], 0.125), "hsxad", 1)
                for mb in range(16):
                    j = gi * 16 + mb
                    sc.wait_ge(P.sem("sbfc"), j + 1)
                    P.bump(sc.activation(ssq[j % 4][:, :], sbf[j % 4][:, :], AF.Square),
                           "ssqa", 1)
                sc.wait_ge(P.sem("stmm"), 2 * 16 * (gi + 1))
                if gi > 0:
                    sc.wait_ge(P.sem("x1d"), 16 * gi)
                sc.mul(mu_sb[:, :], ps1[:, :], 1.0 / D)
                sc.mul(var_sb[:, :], ps2[:, :], 1.0 / D)
                P.bump(sc.activation(rstd_sb[:, :], mu_sb[:, :], AF.Square), "aln0", 1)
                sc.wait_ge(P.sem("vvar"), gi + 1)
                sc.activation(var_sb[:, :], var_sb[:, :], AF.Ln, bias=epsc[:, :])
                P.bump(sc.activation(rstd_sb[:, :], var_sb[:, :], AF.Exp, scale=-0.5),
                       "aln", 1)

        @blk.vector
        def v0_(v):
            for gi in range(NG):
                v.wait_ge(P.sem("hxamm"), gi + 1)
                P.bump(v.tensor_tensor(hxw[:, :], ps_l[0:64, 0:tg], W64g[:, :], ALU.mult),
                       "hxwd", 1)
                for mb in range(16):
                    j = gi * 16 + mb
                    v.wait_ge(P.sem("mm"), j + 1)
                    v.wait_ge(P.sem(f"xrld{j % 2}"), 16 * (j // 2 + 1))
                    col = mb * tg
                    ins = v.tensor_tensor(s_sb[:, col:col + 512], pm[j % 2][:, :],
                                          xres[j % 2][:, :], ALU.add)
                    P.bump(ins, "sres", 1)
                    if j >= 4:
                        v.wait_ge(P.sem("stmm"), 2 * (j - 3))
                    P.bump(v.tensor_copy(sbf[j % 4][:, :], s_sb[:, col:col + 512]),
                           "sbfc", 1)
                v.wait_ge(P.sem("aln0"), gi + 1)
                P.bump(v.tensor_tensor(var_sb[:, :], var_sb[:, :], rstd_sb[:, :],
                                       ALU.subtract), "vvar", 1)
                v.wait_ge(P.sem("aln"), gi + 1)
                for mb in range(16):
                    j2 = gi * 16 + mb
                    col = mb * tg
                    if j2 >= 2:
                        v.wait_ge(P.sem("x1st"), 16 * (j2 - 1))
                    v.tensor_tensor(x1t[j2 % 2][:, :], s_sb[:, col:col + 512],
                                    mu_sb[:, 0:512], ALU.subtract)
                    P.bump(v.tensor_tensor(x1t[j2 % 2][:, :], x1t[j2 % 2][:, :],
                                           rstd_sb[:, 0:512], ALU.mult), "x1d", 1)

        @blk.gpsimd
        def g1_(g):
            P.wait(g, "x1st")
    return P


def run_phase_p3b(prep, hid_arrs, w64_arrs, x1_arrs, fused=False):
    P = Prog(fused)
    hid = P.inp("hid", hid_arrs, BF16)
    W64b = P.inp("W64b", w64_arrs, BF16)
    x1 = P.inp("x1", x1_arrs, F32)
    w_down_p = P.inp("l0_w_down_p", prep["l0_w_down_p"])
    sA_down = P.inp("l0_sA_down", prep["l0_sA_down"])
    A_down_all = P.inp("l0_A_down_all", prep["l0_A_down_all"])
    sB_down = P.inp("l0_sB_down", prep["l0_sB_down"])
    B_down_all = P.inp("l0_B_down_all", prep["l0_B_down_all"])
    x2 = P.outp("x2", (D, T), F32)
    build_p3b(P, hid, W64b, x1, w_down_p, sA_down, A_down_all, sB_down, B_down_all, x2)
    return P.run()



def build_p4(P, x2, qk_p, v_fm, gamma, beta, qb, kself, vself):
    nc = P.nc
    TGL = 256
    NGL = T // TGL
    x2g = [P.sb(f"q_x2g{j}", [128, 16 * TGL], F32) for j in range(2)]
    xn2 = P.sb("q_xn", [128, 16 * T], BF16)
    vw = P.sb("q_vw", [128, 16 * D], BF16)
    wbuf = [P.sb(f"q_w{j}", [128, 2048], BF16) for j in range(2)]
    gam = P.sb("q_gam", [128, 16], F32)
    bet = P.sb("q_bet", [128, 16], F32)
    ones128b = P.sb("q_o128", [128, 128], BF16)
    epsc = P.sb("q_eps", [128, 1], F32)
    sbf = [P.sb(f"q_sbf{j}", [128, TGL], BF16) for j in range(4)]
    ssq = [P.sb(f"q_ssq{j}", [128, TGL], BF16) for j in range(4)]
    mu_sb = P.sb("q_mu", [128, TGL], F32)
    var_sb = P.sb("q_var", [128, TGL], F32)
    rstd_sb = P.sb("q_rstd", [128, TGL], F32)
    ob = [P.sb(f"q_ob{j}", [128, 512], BF16) for j in range(4)]
    pm = [P.ps(f"q_pm{j}", [128, 512], F32) for j in range(2)]
    ps1 = P.ps("q_ps1", [128, TGL], F32)
    ps2 = P.ps("q_ps2", [128, TGL], F32)

    with nc.Block() as blk:
        @blk.gpsimd
        def g0_(g):
            g.memset(epsc[:, :], EPS_LN)
            P.bump(g.dma_start(out=gam[:, :], in_=gamma[:, :]), "ildg", 16)
            P.bump(g.dma_start(out=bet[:, :], in_=beta[:, :]), "ildg", 16)
            P.bump(g.dma_start(out=vw[:, :], in_=v_fm[:, :]), "ildg", 16)
            P.bump(g.memset(ones128b[:, :], 1.0), "icst", 1)
            for gi in range(NGL):
                if gi >= 2:
                    g.wait_ge(P.sem("xnd"), 16 * (gi - 1))
                for mb in range(16):
                    P.bump(g.dma_start(out=x2g[gi % 2][:, mb * TGL:(mb + 1) * TGL],
                                       in_=x2[mb * 128:(mb + 1) * 128, gi * TGL:(gi + 1) * TGL]),
                           f"x2ld{gi % 2}", 16)
            for mb in range(32):
                if mb >= 2:
                    g.wait_ge(P.sem("qkevac"), 4 * (mb - 1))
                P.bump(g.dma_start(out=wbuf[mb % 2][:, :], in_=qk_p[mb]), f"wld{mb % 2}", 16)

        for gi in range(NGL):
            @blk.vector
            def vl_(v, gi=gi):
                v.wait_ge(P.sem(f"x2ld{gi % 2}"), 16 * 16 * (gi // 2 + 1))
                if gi == 0:
                    P.wait(v, "icst")
                    P.wait(v, "ildg")
                for mb in range(16):
                    j = gi * 16 + mb
                    if j >= 4:
                        v.wait_ge(P.sem("stmm"), 2 * (j - 3))
                    P.bump(v.tensor_copy(sbf[j % 4][:, :], x2g[gi % 2][:, mb * TGL:(mb + 1) * TGL]),
                           "sbfc", 1)
                v.wait_ge(P.sem("aln0"), gi + 1)
                P.bump(v.tensor_tensor(var_sb[:, :], var_sb[:, :], rstd_sb[:, :],
                                       ALU.subtract), "vvar", 1)
                v.wait_ge(P.sem("aln"), gi + 1)
                for mb in range(16):
                    v.tensor_tensor(x2g[gi % 2][:, mb * TGL:(mb + 1) * TGL],
                                    x2g[gi % 2][:, mb * TGL:(mb + 1) * TGL], mu_sb[:, :], ALU.subtract)
                    v.tensor_tensor(x2g[gi % 2][:, mb * TGL:(mb + 1) * TGL],
                                    x2g[gi % 2][:, mb * TGL:(mb + 1) * TGL], rstd_sb[:, :], ALU.mult)
                    P.bump(v.tensor_scalar(xn2[:, mb * T + gi * TGL:mb * T + (gi + 1) * TGL],
                                           x2g[gi % 2][:, mb * TGL:(mb + 1) * TGL],
                                           gam[:, mb:mb + 1], bet[:, mb:mb + 1],
                                           ALU.mult, ALU.add), "xnd", 1)

            @blk.scalar
            def al_(sc, gi=gi):
                for mb in range(16):
                    j = gi * 16 + mb
                    sc.wait_ge(P.sem("sbfc"), j + 1)
                    P.bump(sc.activation(ssq[j % 4][:, :], sbf[j % 4][:, :], AF.Square),
                           "ssqa", 1)
                sc.wait_ge(P.sem("stmm"), 2 * 16 * (gi + 1))
                if gi > 0:
                    sc.wait_ge(P.sem("xnd"), 16 * gi)
                sc.mul(mu_sb[:, :], ps1[:, :], 1.0 / D)
                sc.mul(var_sb[:, :], ps2[:, :], 1.0 / D)
                P.bump(sc.activation(rstd_sb[:, :], mu_sb[:, :], AF.Square), "aln0", 1)
                sc.wait_ge(P.sem("vvar"), gi + 1)
                sc.activation(var_sb[:, :], var_sb[:, :], AF.Ln, bias=epsc[:, :])
                P.bump(sc.activation(rstd_sb[:, :], var_sb[:, :], AF.Exp, scale=-0.5),
                       "aln", 1)

            @blk.tensor
            def tl_(t, gi=gi):
                if gi == 0:
                    P.wait(t, "icst")
                if gi > 0:
                    t.wait_ge(P.sem("aln"), gi)
                for mb in range(16):
                    j = gi * 16 + mb
                    t.wait_ge(P.sem("ssqa"), j + 1)
                    t.matmul(ps1[:, :], ones128b[:, :], sbf[j % 4][:, :],
                             start=(mb == 0), stop=(mb == 15))
                    ins = t.matmul(ps2[:, :], ones128b[:, :], ssq[j % 4][:, :],
                                   start=(mb == 0), stop=(mb == 15))
                    P.bump(ins, "stmm", 2)

        @blk.tensor
        def tq_(t):
            P.wait(t, "xnd")
            for mb in range(32):
                t.wait_ge(P.sem(f"wld{mb % 2}"), 16 * (mb // 2 + 1))
                for tt in range(4):
                    j = mb * 4 + tt
                    if j >= 2:
                        t.wait_ge(P.sem("qkevac"), j - 1)
                    for kc in range(16):
                        ins = t.matmul(pm[j % 2][:, :], wbuf[mb % 2][:, kc * 128:(kc + 1) * 128],
                                       xn2[:, kc * T + tt * 512:kc * T + tt * 512 + 512],
                                       start=(kc == 0), stop=(kc == 15))
                    P.bump(ins, "qkmm", 1)
            for tb in range(16):
                for vb in range(4):
                    j = 128 + tb * 4 + vb
                    if j >= 2:
                        t.wait_ge(P.sem("qkevac"), j - 1)
                    for kc in range(16):
                        ins = t.matmul(pm[j % 2][:, :],
                                       xn2[:, kc * T + tb * 128:kc * T + (tb + 1) * 128],
                                       vw[:, kc * D + vb * 512:kc * D + vb * 512 + 512],
                                       start=(kc == 0), stop=(kc == 15))
                    P.bump(ins, "qkmm", 1)

        @blk.vector
        def vq_(v):
            for j in range(128 + 64):
                v.wait_ge(P.sem("qkmm"), j + 1)
                if j >= 4:
                    v.wait_ge(P.sem(f"qkst{(j % 4)}"), 16 * (j // 4))
                P.bump(v.tensor_copy(ob[j % 4][:, :], pm[j % 2][:, :]), "qkevac", 1)

        @blk.sync
        def sq_(s):
            for mb in range(32):
                for tt in range(4):
                    j = mb * 4 + tt
                    s.wait_ge(P.sem("qkevac"), j + 1)
                    if mb < 16:
                        dst = qb[mb * 128:(mb + 1) * 128, tt * 512:(tt + 1) * 512]
                    else:
                        dst = kself[(mb - 16) * 128:(mb - 15) * 128, tt * 512:(tt + 1) * 512]
                    P.bump(s.dma_start(out=dst, in_=ob[j % 4][:, :]), f"qkst{j % 4}", 16)
            for tb in range(16):
                for vb in range(4):
                    j = 128 + tb * 4 + vb
                    s.wait_ge(P.sem("qkevac"), j + 1)
                    P.bump(s.dma_start(out=vself[tb * 128:(tb + 1) * 128, vb * 512:(vb + 1) * 512],
                                       in_=ob[j % 4][:, :]), f"qkst{j % 4}", 16)

        @blk.gpsimd
        def gf_(g):
            for j in range(4):
                P.wait(g, f"qkst{j}")
    return P


def run_phase_p4(prep, x2_arrs, fused=False):
    P = Prog(fused)
    x2 = P.inp("x2", x2_arrs, F32)
    qk_p = P.inp("l1_qk_p", prep["l1_qk_p"])
    v_fm = P.inp("l1_v_fm", prep["l1_v_fm"])
    gamma = P.inp("l1_gamma", prep["l1_gamma"])
    beta = P.inp("l1_beta", prep["l1_beta"])
    qb = P.outp("qb", (D, T), BF16)
    kself = P.outp("kself", (D, T), BF16)
    vself = P.outp("vself", (T, D), BF16)
    build_p4(P, x2, qk_p, v_fm, gamma, beta, qb, kself, vself)
    return P.run()



def KT_OF(qi):
    return (qi + 20) // 4


def build_p5(P, qb, k_lo, v_lo, k_hi, v_hi, h_val, aof):
    nc = P.nc
    q_sb = P.sb("t_q", [128, 16 * 128], BF16)
    ktile = [P.sb(f"t_k{j}", [128, 16 * 512], BF16) for j in range(2)]
    vtile = [P.sb(f"t_v{j}", [128, 4 * 2048], BF16) for j in range(2)]
    iotJ = P.sb("t_J", [128, 512], F32)
    idb = P.sb("t_idb", [128, 128], BF16)
    iot128 = P.sb("t_i128", [128, 128], F32)
    hcol = P.sb("t_h", [128, 1], F32)
    acc = P.sb("t_acc", [128, 2048], F32)
    m_t = P.sb("t_m", [128, 1], F32)
    l_t = P.sb("t_l", [128, 1], F32)
    rmax = P.sb("t_rmax", [128, 1], F32)
    mn_t = P.sb("t_mn", [128, 1], F32)
    ma_t = P.sb("t_ma", [128, 1], F32)
    nm_t = P.sb("t_nm", [128, 1], F32)
    alpha = P.sb("t_al", [128, 1], F32)
    tsum = P.sb("t_ts", [128, 1], F32)
    rcol = P.sb("t_rcol", [128, 1], F32)
    mtile = P.sb("t_mt", [128, 512], F32)
    sc_sb = P.sb("t_sc", [128, 512], F32)
    p_bf = P.sb("t_p", [128, 512], BF16)
    PT_sb = P.sb("t_pt", [128, 512], BF16)
    aob = P.sb("t_aob", [128, 2048], BF16)
    aofm = P.sb("t_aofm", [128, 16 * 128], BF16)
    ps_sc = P.ps("t_pssc", [128, 512], F32)
    ps_pt = P.ps("t_pspt", [128, 512], BF16)
    ps_av = [P.ps(f"t_psav{j}", [128, 512], F32) for j in range(2)]

    pairs_before = [0]
    for qi in range(16):
        pairs_before.append(pairs_before[-1] + KT_OF(qi))

    kldval = {}
    cnt = {0: 0, 1: 0}
    for qi in range(16):
        for kt in range(KT_OF(qi)):
            par = kt % 2
            cnt[par] += 32
            kldval[(qi, kt)] = cnt[par]

    def vse(v, ins):
        n = P.bump(ins, "vself", 1)
        v.wait_ge(P.sem("vself"), n)

    SCALE = 1.0 / float(np.sqrt(2048.0))

    with nc.Block() as blk:
        @blk.gpsimd
        def g0_(g):
            P.bump(g.dma_start(out=hcol[:, :], in_=h_val[:, :]), "ildg", 16)
            g.iota(iotJ[:, :], [[1, 512]], channel_multiplier=-1,
                   allow_small_or_imprecise_dtypes=True)
            g.iota(iot128[:, :], [[1, 128]], channel_multiplier=-1,
                   allow_small_or_imprecise_dtypes=True)
            P.bump(g.nop(), "icst", 1)

        @blk.vector
        def v0_(v):
            P.wait(v, "icst")
            P.wait(v, "ildg")
            P.bump(v.tensor_scalar(idb[:, :], iot128[:, :], 0.0, None, ALU.is_equal), "cst1", 1)

        for qi in range(16):
            NKT = KT_OF(qi)
            pb0 = pairs_before[qi]

            @blk.sync
            def ld_(s, qi=qi, NKT=NKT, pb0=pb0):
                if qi > 0:
                    s.wait_ge(P.sem("aexp5"), pb0)
                P.bump(s.dma_start(
                    out=q_sb[:, :].rearrange("p (kc t) -> p kc t", kc=16),
                    in_=qb[:, qi * 128:(qi + 1) * 128].rearrange("(kc p) t -> p kc t", p=128)),
                    "qld", 16)
                for kt in range(NKT):
                    pr = pb0 + kt
                    if pr >= 2:
                        s.wait_ge(P.sem("aexp5"), pr - 1)
                        s.wait_ge(P.sem("dacc"), 4 * (pr - 1))
                    ksrc = k_lo if kt < 4 else k_hi
                    vsrc = v_lo if kt < 4 else v_hi
                    co = (kt % 4) * 512
                    P.bump(s.dma_start(
                        out=ktile[kt % 2][:, :].rearrange("p (kc t) -> p kc t", kc=16),
                        in_=ksrc[:, co:co + 512].rearrange("(kc p) t -> p kc t", p=128)),
                        f"kld{kt % 2}", 16)
                    P.bump(s.dma_start(
                        out=vtile[kt % 2][:, :].rearrange("p (tc d) -> p tc d", tc=4),
                        in_=vsrc[co:co + 512, :].rearrange("(tc p) d -> p tc d", p=128)),
                        f"kld{kt % 2}", 16)

            @blk.vector
            def vi_(v, qi=qi):
                if qi > 0:
                    v.wait_ge(P.sem("aost"), 16 * qi)
                v.memset(acc[:, :], 0.0)
                v.memset(m_t[:, :], -3.0e38)
                P.bump(v.memset(l_t[:, :], 0.0), "finit", 1)

            @blk.tensor
            def tq_(t, qi=qi, NKT=NKT, pb0=pb0):
                if qi == 0:
                    P.wait(t, "cst1")
                t.wait_ge(P.sem("qld"), 16 * (qi + 1))
                for kt in range(NKT):
                    pr = pb0 + kt
                    t.wait_ge(P.sem(f"kld{kt % 2}"), kldval[(qi, kt)])
                    if pr >= 1:
                        t.wait_ge(P.sem("aexp5"), pr)
                    for kc in range(16):
                        ins = t.matmul(ps_sc[:, :], q_sb[:, kc * 128:(kc + 1) * 128],
                                       ktile[kt % 2][:, kc * 512:(kc + 1) * 512],
                                       start=(kc == 0), stop=(kc == 15))
                    P.bump(ins, "smm5", 1)
                    t.wait_ge(P.sem("aexp5"), pr + 1)
                    if pr >= 1:
                        t.wait_ge(P.sem("dpt"), pr)
                    for t2 in range(4):
                        ins = t.transpose(ps_pt[:, t2 * 128:(t2 + 1) * 128],
                                          p_bf[:, t2 * 128:(t2 + 1) * 128], idb[:, :])
                    P.bump(ins, "ptr5", 1)
                    t.wait_ge(P.sem("dpt"), pr + 1)
                    for dg in range(4):
                        if 4 * pr + dg >= 2:
                            t.wait_ge(P.sem("dacc"), 4 * pr + dg - 1)
                        for ktc in range(4):
                            ins = t.matmul(ps_av[dg % 2][:, :],
                                           PT_sb[:, ktc * 128:(ktc + 1) * 128],
                                           vtile[kt % 2][:, ktc * 2048 + dg * 512:
                                                         ktc * 2048 + dg * 512 + 512],
                                           start=(ktc == 0), stop=(ktc == 3))
                        P.bump(ins, "avmm", 1)
                t.wait_ge(P.sem("aobd"), qi + 1)
                for r in range(4):
                    rr = 4 * qi + r
                    if rr >= 1:
                        t.wait_ge(P.sem("aofmd"), rr)
                    for t2 in range(4):
                        ins = t.transpose(ps_pt[:, t2 * 128:(t2 + 1) * 128],
                                          aob[:, r * 512 + t2 * 128:r * 512 + (t2 + 1) * 128],
                                          idb[:, :])
                    P.bump(ins, "aotr", 1)

            @blk.scalar
            def aq_(sc, qi=qi, NKT=NKT, pb0=pb0):
                for kt in range(NKT):
                    pr = pb0 + kt
                    r_e = qi * 128 - kt * 512
                    masked = r_e < 512
                    sc.wait_ge(P.sem("dexp_rdy"), pr + 1)
                    src = sc_sb[:, :] if masked else ps_sc[:, :]
                    P.bump(sc.activation(p_bf[:, :], src, AF.Exp, bias=nm_t[:, :],
                                         scale=SCALE, accum_out=tsum[:, :]), "aexp5", 1)
                    P.bump(sc.activation(alpha[:, :], ma_t[:, :], AF.Exp, scale=SCALE),
                           "aalpha", 1)

            @blk.vector
            def vq_(v, qi=qi, NKT=NKT, pb0=pb0):
                for kt in range(NKT):
                    pr = pb0 + kt
                    r_e = qi * 128 - kt * 512
                    masked = r_e < 512
                    v.wait_ge(P.sem("smm5"), pr + 1)
                    if masked:
                        vse(v, v.tensor_scalar(rcol[:, :], hcol[:, :], 2048.0, float(r_e),
                                               ALU.mult, ALU.add))
                        vse(v, v.tensor_scalar(mtile[:, :], iotJ[:, :], rcol[:, :], None,
                                               ALU.is_gt))
                        vse(v, v.scalar_tensor_tensor(sc_sb[:, :], mtile[:, :], -1.0e30,
                                                      ps_sc[:, :], ALU.mult, ALU.add))
                        vse(v, v.reduce_max(rmax[:, :], sc_sb[:, :], axis=AX.X))
                    else:
                        vse(v, v.reduce_max(rmax[:, :], ps_sc[:, :], axis=AX.X))
                    vse(v, v.tensor_tensor(mn_t[:, :], m_t[:, :], rmax[:, :], ALU.max))
                    vse(v, v.tensor_tensor(ma_t[:, :], m_t[:, :], mn_t[:, :], ALU.subtract))
                    vse(v, v.tensor_scalar(nm_t[:, :], mn_t[:, :], -SCALE, None, ALU.mult))
                    n = P.bump(v.tensor_copy(m_t[:, :], mn_t[:, :]), "dexp_rdy", 1)
                    v.wait_ge(P.sem("dexp_rdy"), n)
                    v.wait_ge(P.sem("aalpha"), pr + 1)
                    vse(v, v.scalar_tensor_tensor(l_t[:, :], l_t[:, :], alpha[:, :],
                                                  tsum[:, :], ALU.mult, ALU.add))
                    v.wait_ge(P.sem("ptr5"), pr + 1)
                    P.bump(v.tensor_copy(PT_sb[:, :], ps_pt[:, :]), "dpt", 1)
                    for dg in range(4):
                        v.wait_ge(P.sem("avmm"), 4 * pr + dg + 1)
                        n = P.bump(v.scalar_tensor_tensor(acc[:, dg * 512:(dg + 1) * 512],
                                                          acc[:, dg * 512:(dg + 1) * 512],
                                                          alpha[:, :], ps_av[dg % 2][:, :],
                                                          ALU.mult, ALU.add), "dacc", 1)
                        v.wait_ge(P.sem("dacc"), n)
                vse(v, v.reciprocal(rcol[:, :], l_t[:, :]))
                P.bump(v.tensor_scalar(aob[:, :], acc[:, :], rcol[:, :], None, ALU.mult),
                       "aobd", 1)
                for r in range(4):
                    rr = 4 * qi + r
                    v.wait_ge(P.sem("aotr"), rr + 1)
                    P.bump(v.tensor_copy(aofm[:, r * 512:(r + 1) * 512], ps_pt[:, :]),
                           "aofmd", 1)

            @blk.gpsimd
            def gs_(g, qi=qi):
                g.wait_ge(P.sem("aofmd"), 4 * (qi + 1))
                P.bump(g.dma_start(
                    out=aof[:, qi * 128:(qi + 1) * 128].rearrange("(kc p) t -> p kc t", p=128),
                    in_=aofm[:, :].rearrange("p (kc t) -> p kc t", kc=16)), "aost", 16)

        @blk.gpsimd
        def gf_(g):
            P.wait(g, "aost")
    return P


def run_phase_p5(prep, qb_arrs, klo, vlo, khi, vhi, fused=False):
    P = Prog(fused)
    qb = P.inp("qb", qb_arrs, BF16)
    k_lo = P.inp("k_lo", klo, BF16)
    v_lo = P.inp("v_lo", vlo, BF16)
    k_hi = P.inp("k_hi", khi, BF16)
    v_hi = P.inp("v_hi", vhi, BF16)
    h_val = P.inp("h_val", prep["h_val"])
    aof = P.outp("aof", (D, T), BF16)
    build_p5(P, qb, k_lo, v_lo, k_hi, v_hi, h_val, aof)
    return P.run()


def run_phase_p5b(prep, aof_arrs, x2_arrs, fused=False):
    P = Prog(fused)
    aof = P.inp("aof", aof_arrs, BF16)
    x2 = P.inp("x2", x2_arrs, F32)
    oproj_p = P.inp("l1_oproj_p", prep["l1_oproj_p"])
    x3 = P.outp("x3", (D, T), F32)
    build_p2b(P, aof, x2, oproj_p, x3, tg=1024, src_off=0, double_res=True, name="o")
    return P.run()



def build_p6(P, x3, router_p, w1_p, w3_p, w2_p, y_out, NG_LIMIT=None, STAGE=99):
    nc = P.nc
    TG = 512
    NG = T // TG if NG_LIMIT is None else NG_LIMIT
    xn = P.sb("m_xn", [128, 16 * TG], BF16)
    h_all = P.sb("m_h", [128, 8 * 4 * 512], BF16)
    router_sb = P.sb("m_rt", [128, 16 * 8], BF16)
    w13buf = [P.sb(f"m_w13_{j}", [128, 2048], BF16) for j in range(4)]
    w2buf = [P.sb(f"m_w2_{j}", [128, 512], BF16) for j in range(2)]
    iot = P.sb("m_iot", [128, 128], F32)
    idf = P.sb("m_idf", [128, 128], F32)
    sel8 = P.sb("m_sel8", [8, 8 * 128], F32)
    iotasel = P.sb("m_iotsel", [8, 128], F32)
    scores_fm = P.sb("m_scf", [8, TG], F32)
    s_tok = P.sb("m_stok", [128, 32], F32)
    m1 = P.sb("m_m1", [128, 1], F32)
    m2 = P.sb("m_m2", [128, 1], F32)
    eqA = P.sb("m_eqA", [128, 8], F32)
    eqB = P.sb("m_eqB", [128, 8], F32)
    b2 = P.sb("m_b2", [128, 8], F32)
    d12 = P.sb("m_d12", [128, 1], F32)
    w1c = P.sb("m_w1c", [128, 1], F32)
    w2c = P.sb("m_w2c", [128, 1], F32)
    t8 = P.sb("m_t8", [128, 8], F32)
    w_tokf = [P.sb(f"m_wtok{j}", [128, 8], F32) for j in range(2)]
    w_fm = P.sb("m_wfm", [8, TG], F32)
    sg = P.sb("m_sg", [128, 512], BF16)
    xres = [P.sb(f"m_xr{j}", [128, 512], F32) for j in range(2)]
    s_sb = P.sb("m_s", [128, 16 * TG], F32)
    sbf = [P.sb(f"m_sbf{j}", [128, TG], BF16) for j in range(4)]
    ssq = [P.sb(f"m_ssq{j}", [128, TG], BF16) for j in range(4)]
    ones128b = P.sb("m_o128", [128, 128], BF16)
    epsc = P.sb("m_eps", [128, 1], F32)
    mu_sb = P.sb("m_mu", [128, TG], F32)
    var_sb = P.sb("m_var", [128, TG], F32)
    rstd_sb = P.sb("m_rstd", [128, TG], F32)
    x4t = [P.sb(f"m_x4t{j}", [128, 512], F32) for j in range(2)]
    x4o = [P.sb(f"m_x4o{j}", [128, 512], F32) for j in range(2)]
    pg = [P.ps(f"m_pg{j}", [128, 512], F32) for j in range(4)]
    ps_sm = P.ps("m_pssm", [128, 512], F32)
    psx4 = [P.ps(f"m_px{j}", [128, 128], F32) for j in range(2)]

    def vse(v, ins):
        n = P.bump(ins, "vself", 1)
        v.wait_ge(P.sem("vself"), n)

    with nc.Block() as blk:
        @blk.gpsimd
        def g0_(g):
            g.memset(epsc[:, :], EPS_LN)
            g.iota(iot[:, :], [[1, 128]], channel_multiplier=-1,
                   allow_small_or_imprecise_dtypes=True)
            g.iota(iotasel[:, :], [[0, 128]], channel_multiplier=1,
                   allow_small_or_imprecise_dtypes=True)
            P.bump(g.dma_start(out=router_sb[:, :], in_=router_p[:, :]), "ildg", 16)
            P.bump(g.memset(ones128b[:, :], 1.0), "icst", 1)
            for gi in range(NG):
                if gi > 0:
                    g.wait_ge(P.sem("hd"), 32 * gi)
                for kc in range(16):
                    P.bump(g.dma_start(out=xn[:, kc * TG:(kc + 1) * TG],
                                       in_=x3[kc * 128:(kc + 1) * 128, gi * TG:(gi + 1) * TG]),
                           f"xnld{gi % 2}", 16)
                if STAGE >= 2:
                    for e in range(8):
                        for fb in range(4):
                            k = (gi * 8 + e) * 4 + fb
                            if k >= 2:
                                g.wait_ge(P.sem("hd"), k - 1)
                            P.bump(g.dma_start(out=w13buf[(k % 2) * 2][:, :], in_=w1_p[e, fb]),
                                   f"w1ld{k % 2}", 16)
                            P.bump(g.dma_start(out=w13buf[(k % 2) * 2 + 1][:, :], in_=w3_p[e, fb]),
                                   f"w3ld{k % 2}", 16)
                if STAGE >= 3:
                    for db in range(16):
                        for e in range(8):
                            k2 = (gi * 16 + db) * 8 + e
                            if k2 >= 2:
                                g.wait_ge(P.sem("dmm6"), k2 - 1)
                            P.bump(g.dma_start(out=w2buf[k2 % 2][:, :], in_=w2_p[e, db]),
                                   f"w2ld{k2 % 2}", 16)

        @blk.vector
        def v0_(v):
            P.wait(v, "icst")
            P.wait(v, "ildg")
            v.tensor_scalar(idf[:, :], iot[:, :], 0.0, None, ALU.is_equal)
            for e in range(8):
                ins = v.tensor_scalar(sel8[:, e * 128:(e + 1) * 128], iotasel[:, :],
                                      float(e), None, ALU.is_equal)
            P.bump(ins, "cst1", 1)

        for gi in range(NG):
            @blk.tensor
            def tr_(t, gi=gi):
                P.wait(t, "cst1")
                t.wait_ge(P.sem(f"xnld{gi % 2}"), 16 * 16 * (gi // 2 + 1))
                if gi > 0:
                    t.wait_ge(P.sem("wfmd"), gi)
                    t.wait_ge(P.sem("x4od"), 64 * gi)
                for kc in range(16):
                    ins = t.matmul(ps_sm[0:8, 0:TG], router_sb[:, kc * 8:(kc + 1) * 8],
                                   xn[:, kc * TG:(kc + 1) * TG], start=(kc == 0), stop=(kc == 15))
                P.bump(ins, "rmm", 1)

            @blk.vector
            def vscf_(v, gi=gi):
                P.wait(v, "rmm")
                P.bump(v.tensor_copy(scores_fm[:, :], ps_sm[0:8, 0:TG]), "rscf", 1)

            @blk.tensor
            def tr2_(t, gi=gi):
                P.wait(t, "rscf")
                for t2 in range(4):
                    ins = t.transpose(ps_sm[:, 256 + t2 * 8:256 + (t2 + 1) * 8],
                                      scores_fm[:, t2 * 128:(t2 + 1) * 128], idf[0:8, 0:8])
                P.bump(ins, "rtr", 1)

            @blk.vector
            def vr_(v, gi=gi):
                P.wait(v, "rtr")
                v.tensor_copy(s_tok[:, :], ps_sm[:, 256:288])
                for t2 in range(4):
                    jt = gi * 4 + t2
                    st = s_tok[:, t2 * 8:(t2 + 1) * 8]
                    vse(v, v.reduce_max(m1[:, :], st, axis=AX.X))
                    vse(v, v.tensor_scalar(eqA[:, :], st, m1[:, :], None, ALU.is_equal))
                    vse(v, v.scalar_tensor_tensor(b2[:, :], eqA[:, :], -1e30, st,
                                                  ALU.mult, ALU.add))
                    vse(v, v.reduce_max(m2[:, :], b2[:, :], axis=AX.X))
                    vse(v, v.tensor_scalar(eqB[:, :], st, m2[:, :], None, ALU.is_equal))
                    n = P.bump(v.tensor_tensor(d12[:, :], m1[:, :], m2[:, :], ALU.subtract),
                               "d12d", 1)
                    v.wait_ge(P.sem("d12d"), n)
                    v.wait_ge(P.sem("aw12"), jt + 1)
                    vse(v, v.tensor_scalar(t8[:, :], eqA[:, :], w1c[:, :], None, ALU.mult))
                    if jt >= 2:
                        v.wait_ge(P.sem("wtr6"), jt - 1)
                    ins = v.scalar_tensor_tensor(w_tokf[t2 % 2][:, :], eqB[:, :], w2c[:, :],
                                                 t8[:, :], ALU.mult, ALU.add)
                    P.bump(ins, "wtok6", 1)

            @blk.scalar
            def aw_(sc, gi=gi):
                for t2 in range(4):
                    jt = gi * 4 + t2
                    sc.wait_ge(P.sem("d12d"), jt + 1)
                    sc.activation(w1c[:, :], d12[:, :], AF.Sigmoid)
                    P.bump(sc.activation(w2c[:, :], d12[:, :], AF.Sigmoid, scale=-1.0),
                           "aw12", 1)

            @blk.tensor
            def tr3_(t, gi=gi):
                for t2 in range(4):
                    jt = gi * 4 + t2
                    t.wait_ge(P.sem("wtok6"), jt + 1)
                    ins = t.transpose(ps_sm[0:8, t2 * 128:(t2 + 1) * 128],
                                      w_tokf[t2 % 2][:, :], idf[:, :])
                    P.bump(ins, "wtr6", 1)

            @blk.vector
            def vr2_(v, gi=gi):
                v.wait_ge(P.sem("wtr6"), 4 * (gi + 1))
                P.bump(v.tensor_copy(w_fm[:, :], ps_sm[0:8, 0:TG]), "wfmd", 1)

            @blk.tensor
            def te_(t, gi=gi):
                if STAGE < 2:
                    return
                P.wait(t, "wfmd")
                if gi > 0:
                    t.wait_ge(P.sem("aln"), gi)
                    t.wait_ge(P.sem("sres6"), 16 * gi)
                for e in range(8):
                    if gi > 0 or e > 0:
                        t.wait_ge(P.sem("hd"), 32 * gi + 4 * e)
                    P.bump(t.matmul(ps_sm[:, :], sel8[:, e * 128:(e + 1) * 128], w_fm[:, :],
                                    start=True, stop=True), "wrepmm", 1)
                    for fb in range(4):
                        k = (gi * 8 + e) * 4 + fb
                        t.wait_ge(P.sem(f"w1ld{k % 2}"), 16 * (k // 2 + 1))
                        t.wait_ge(P.sem(f"w3ld{k % 2}"), 16 * (k // 2 + 1))
                        if k >= 2:
                            t.wait_ge(P.sem("hd"), k - 1)
                        for kc in range(16):
                            ins = t.matmul(pg[k % 2][:, :],
                                           w13buf[(k % 2) * 2][:, kc * 128:(kc + 1) * 128],
                                           xn[:, kc * TG:(kc + 1) * TG],
                                           start=(kc == 0), stop=(kc == 15))
                        P.bump(ins, "mm1", 1)
                        for kc in range(16):
                            ins = t.matmul(pg[2 + k % 2][:, :],
                                           w13buf[(k % 2) * 2 + 1][:, kc * 128:(kc + 1) * 128],
                                           xn[:, kc * TG:(kc + 1) * TG], start=(kc == 0),
                                           stop=(kc == 15))
                        P.bump(ins, "mm3", 1)

            @blk.scalar
            def ae_(sc, gi=gi):
                if STAGE < 2:
                    return
                for e in range(8):
                    for fb in range(4):
                        k = (gi * 8 + e) * 4 + fb
                        sc.wait_ge(P.sem("mm1"), k + 1)
                        if k >= 1:
                            sc.wait_ge(P.sem("hd"), k)
                        P.bump(sc.activation(sg[:, :], pg[k % 2][:, :], AF.Silu), "asg", 1)

            @blk.vector
            def ve_(v, gi=gi):
                if STAGE < 2:
                    return
                if gi > 0:
                    v.wait_ge(P.sem("dmm6"), 128 * gi)
                for e in range(8):
                    v.wait_ge(P.sem("wrepmm"), gi * 8 + e + 1)
                    for fb in range(4):
                        k = (gi * 8 + e) * 4 + fb
                        v.wait_ge(P.sem("asg"), k + 1)
                        v.wait_ge(P.sem("mm3"), k + 1)
                        col = (e * 4 + fb) * 512
                        vse(v, v.tensor_tensor(h_all[:, col:col + 512], sg[:, :],
                                               pg[2 + k % 2][:, :], ALU.mult))
                        ins = v.tensor_tensor(h_all[:, col:col + 512], h_all[:, col:col + 512],
                                              ps_sm[:, :], ALU.mult)
                        P.bump(ins, "hd", 1)

            @blk.sync
            def sd_(s, gi=gi):
                if STAGE < 3:
                    return
                for db in range(16):
                    j = gi * 16 + db
                    if j >= 2:
                        s.wait_ge(P.sem("sres6"), j - 1)
                    P.bump(s.dma_start(out=xres[j % 2][:, :],
                                       in_=x3[db * 128:(db + 1) * 128, gi * TG:gi * TG + 512]),
                           f"xrld{j % 2}", 16)
                if STAGE < 5:
                    return
                for db in range(16):
                    for r in range(4):
                        j4 = (gi * 16 + db) * 4 + r
                        s.wait_ge(P.sem("x4od"), j4 + 1)
                        P.bump(s.dma_start(
                            out=y_out[gi * TG + r * 128:gi * TG + (r + 1) * 128,
                                      db * 128:(db + 1) * 128],
                            in_=x4o[j4 % 2][:, 0:128]), "yst", 16)

            @blk.tensor
            def td_(t, gi=gi):
                if STAGE < 3:
                    return
                if gi > 0 and STAGE >= 4:
                    t.wait_ge(P.sem("aln"), gi)
                def stats_mm(jl, gi=gi):
                    j = gi * 16 + jl
                    t.wait_ge(P.sem("ssqa"), j + 1)
                    t.matmul(pg[2][:, :], ones128b[:, :], sbf[j % 4][:, :],
                             start=(jl == 0), stop=(jl == 15))
                    ins = t.matmul(pg[3][:, :], ones128b[:, :], ssq[j % 4][:, :],
                                   start=(jl == 0), stop=(jl == 15))
                    P.bump(ins, "stmm", 2)
                for db in range(16):
                    j = gi * 16 + db
                    if j >= 2:
                        t.wait_ge(P.sem("sres6"), j - 1)
                    for e in range(8):
                        k2 = (gi * 16 + db) * 8 + e
                        t.wait_ge(P.sem(f"w2ld{k2 % 2}"), 16 * (k2 // 2 + 1))
                        t.wait_ge(P.sem("hd"), 32 * gi + (e + 1) * 4)
                        for kc in range(4):
                            ins = t.matmul(pg[j % 2][:, :], w2buf[k2 % 2][:, kc * 128:(kc + 1) * 128],
                                           h_all[:, (e * 4 + kc) * 512:(e * 4 + kc + 1) * 512],
                                           start=(e == 0 and kc == 0), stop=(e == 7 and kc == 3))
                        P.bump(ins, "dmm6", 1)
                    if db >= 2 and STAGE >= 4:
                        stats_mm(db - 2)
                if STAGE >= 4:
                    stats_mm(14)
                    stats_mm(15)
                if STAGE < 5:
                    return
                for db in range(16):
                    for r in range(4):
                        j4 = (gi * 16 + db) * 4 + r
                        t.wait_ge(P.sem("x4d"), gi * 16 + db + 1)
                        if j4 >= 2:
                            t.wait_ge(P.sem("x4od"), j4 - 1)
                        ins = t.transpose(psx4[j4 % 2][:, :],
                                          x4t[(gi * 16 + db) % 2][:, r * 128:(r + 1) * 128],
                                          idf[:, :])
                        P.bump(ins, "x4tr", 1)

            @blk.scalar
            def ad_(sc, gi=gi):
                if STAGE < 4:
                    return
                for db in range(16):
                    j = gi * 16 + db
                    sc.wait_ge(P.sem("sbfc6"), j + 1)
                    P.bump(sc.activation(ssq[j % 4][:, :], sbf[j % 4][:, :], AF.Square),
                           "ssqa", 1)
                sc.wait_ge(P.sem("stmm"), 2 * 16 * (gi + 1))
                if gi > 0:
                    sc.wait_ge(P.sem("x4d"), 16 * gi)
                sc.mul(mu_sb[:, :], pg[2][:, :], 1.0 / D)
                sc.mul(var_sb[:, :], pg[3][:, :], 1.0 / D)
                P.bump(sc.activation(rstd_sb[:, :], mu_sb[:, :], AF.Square), "aln0", 1)
                sc.wait_ge(P.sem("vvar"), gi + 1)
                sc.activation(var_sb[:, :], var_sb[:, :], AF.Ln, bias=epsc[:, :])
                P.bump(sc.activation(rstd_sb[:, :], var_sb[:, :], AF.Exp, scale=-0.5),
                       "aln", 1)

            @blk.vector
            def vd_(v, gi=gi):
                if STAGE < 3:
                    return
                for db in range(16):
                    j = gi * 16 + db
                    v.wait_ge(P.sem("dmm6"), 8 * (j + 1))
                    v.wait_ge(P.sem(f"xrld{j % 2}"), 16 * (j // 2 + 1))
                    col = db * TG
                    ins = v.tensor_tensor(s_sb[:, col:col + 512], pg[j % 2][:, :],
                                          xres[j % 2][:, :], ALU.add)
                    P.bump(ins, "sres6", 1)
                    if j >= 4 and STAGE >= 4:
                        v.wait_ge(P.sem("stmm"), 2 * (j - 3))
                    P.bump(v.tensor_copy(sbf[j % 4][:, :], s_sb[:, col:col + 512]), "sbfc6", 1)
                if STAGE < 4:
                    return
                v.wait_ge(P.sem("aln0"), gi + 1)
                P.bump(v.tensor_tensor(var_sb[:, :], var_sb[:, :], rstd_sb[:, :],
                                       ALU.subtract), "vvar", 1)
                v.wait_ge(P.sem("aln"), gi + 1)

                def evacs(db2, gi=gi):
                    for r in range(4):
                        j4 = (gi * 16 + db2) * 4 + r
                        v.wait_ge(P.sem("x4tr"), j4 + 1)
                        if j4 >= 2:
                            v.wait_ge(P.sem("yst"), 16 * (j4 - 1))
                        P.bump(v.tensor_copy(x4o[j4 % 2][:, 0:128],
                                             psx4[j4 % 2][:, :]),
                               "x4od", 1)

                for db in range(16):
                    j = gi * 16 + db
                    col = db * TG
                    if j >= 2 and STAGE >= 5:
                        v.wait_ge(P.sem("x4tr"), 4 * (j - 1))
                    v.tensor_tensor(x4t[j % 2][:, :], s_sb[:, col:col + 512],
                                    mu_sb[:, :], ALU.subtract)
                    P.bump(v.tensor_tensor(x4t[j % 2][:, :], x4t[j % 2][:, :],
                                           rstd_sb[:, :], ALU.mult), "x4d", 1)
                    if db >= 1 and STAGE >= 5:
                        evacs(db - 1)
                if STAGE >= 5:
                    evacs(15)

        if STAGE < 5:
            @blk.sync
            def dbg_(s):
                if STAGE == 1:
                    s.wait_ge(P.sem("wfmd"), NG)
                elif STAGE == 2:
                    s.wait_ge(P.sem("hd"), 32 * NG)
                elif STAGE == 3:
                    s.wait_ge(P.sem("sres6"), 16 * NG)
                elif STAGE == 4:
                    s.wait_ge(P.sem("x4d"), 16 * NG)
                P.bump(s.dma_start(out=y_out[0:8, 0:TG], in_=w_fm[:, :]), "yst", 16)

        @blk.gpsimd
        def gf_(g):
            P.wait(g, "yst")
    return P


def run_phase_p6(prep, x3_arrs, fused=False, NG_LIMIT=None, STAGE=99):
    P = Prog(fused)
    x3 = P.inp("x3", x3_arrs, F32)
    router_p = P.inp("l1_router", prep["l1_router"])
    w1_p = P.inp("l1_w1_p", prep["l1_w1_p"])
    w3_p = P.inp("l1_w3_p", prep["l1_w3_p"])
    w2_p = P.inp("l1_w2_p", prep["l1_w2_p"])
    y_out = P.outp("y_out", (T, D), F32)
    build_p6(P, x3, router_p, w1_p, w3_p, w2_p, y_out, NG_LIMIT=NG_LIMIT, STAGE=STAGE)
    return P.run()



def kernel(**inputs):
    import ml_dtypes
    prep = prep_inputs(inputs)
    r0 = run_phase_p0(prep)
    r0b = run_phase_p0b(prep, [r["xbc"] for r in r0])
    r1 = run_phase_p1(prep, [r["xcb"] for r in r0b], [r["dtf"] for r in r0])
    z64 = np.zeros((64, 4096), ml_dtypes.bfloat16)
    car = [r1[c - 1]["carry_out"] if (c & 1) else z64 for c in range(NCORES)]
    r2a = run_phase_p2a(prep, [r["ypart"] for r in r1], [r["chq"] for r in r1],
                        [r["zb"] for r in r0], [r["lgrow"] for r in r1], car)
    r2b = run_phase_p2b(prep, [r["xg"] for r in r2a])
    x1 = [r["x1"] for r in r2b]
    r3a = run_phase_p3a(prep, x1)
    r3b = run_phase_p3b(prep, [r["hid"] for r in r3a], [r["W64b"] for r in r3a], x1)
    x2 = [r["x2"] for r in r3b]
    r4 = run_phase_p4(prep, x2)
    klo = [r4[(c >> 1) * 2]["kself"] for c in range(NCORES)]
    khi = [r4[(c >> 1) * 2 + 1]["kself"] for c in range(NCORES)]
    vlo = [r4[(c >> 1) * 2]["vself"] for c in range(NCORES)]
    vhi = [r4[(c >> 1) * 2 + 1]["vself"] for c in range(NCORES)]
    r5 = run_phase_p5(prep, [r["qb"] for r in r4], klo, vlo, khi, vhi)
    r5b = run_phase_p5b(prep, [r["aof"] for r in r5], x2)
    r6 = run_phase_p6(prep, [r["x3"] for r in r5b])
    out = np.empty((4, 2 * T, D), np.float32)
    for c in range(NCORES):
        b, h = c >> 1, c & 1
        out[b, h * T:(h + 1) * T, :] = np.asarray(r6[c]["y_out"], np.float32)
    return out
